# version 21
# baseline (speedup 1.0000x reference)
import numpy as np
from contextlib import ExitStack

import concourse.bass as bass
import concourse.mybir as mybir
from concourse.bass_utils import run_bass_kernel_spmd

F32 = mybir.dt.float32
BF16 = mybir.dt.bfloat16
AF = mybir.ActivationFunctionType
ALU = mybir.AluOpType
AX = mybir.AxisListType

D_MODEL = 2048
CONV_DIM = 6144
PROJ_DIM = 8224
INTER = 8192
NCORES = 8
TOK = 2048
NCHUNK = 16
NXT = 48
NPAIR = 16
SUPER = 512
NSUPER = TOK // SUPER
JPAD = 2052

EPS = 1e-5
CORR16 = float((1.0 / (64.0 * np.sqrt(2048.0))) / (2.0 ** -12))
CORR64 = float((1.0 / (64.0 * np.sqrt(8192.0))) / (2.0 ** -12))

PHASES = 99
SPILLS_EXTERNAL = False
DEBUG_SSD = False
UB_LIM = 17
SSD_LIM = 16
SSD_SECT = 5
PAIR_SECT = 9



def ecopy(e, out, in_):
    if hasattr(e, "tensor_copy"):
        return e.tensor_copy(out, in_)
    return e.copy(out, in_)

class Buf:
    __slots__ = ("name", "w", "readers")

    def __init__(self, name):
        self.name = name
        self.w = None
        self.readers = {}


NSLOT = 8


class Prog:
    DMA_STREAMS = {"Dw": 16, "Ds": 16, "Dp": 16, "CC": 1}
    ENGINE_OF = {"T": "tensor", "V": "vector", "S": "scalar", "P": "gpsimd",
                 "CC": "gpsimd", "Dw": "sync", "Ds": "scalar", "Dp": "gpsimd"}

    def __init__(self):
        self.counts = {s: 0 for s in ("T", "V", "S", "P", "CC", "Dw", "Ds", "Dp")}
        self.steps = []
        self.last_wait = {}

    def _emit_wait(self, eng, sems, consumer, s, c):
        if s in self.DMA_STREAMS:
            step = self.DMA_STREAMS[s]
            for r in range(NSLOT):
                if c < r + 1:
                    continue
                i_r = c - ((c - 1 - r) % NSLOT)
                val = step * ((i_r - 1) // NSLOT + 1)
                k = (consumer, s, r)
                if self.last_wait.get(k, 0) < val:
                    eng.wait_ge(sems[(s, r)], val)
                    self.last_wait[k] = val
        else:
            k = (consumer, s)
            if self.last_wait.get(k, 0) < c:
                eng.wait_ge(sems[s], c)
                self.last_wait[k] = c

    def emit(self, stream, fn, reads=(), writes=()):
        need = {}
        for b in reads:
            if b.w is not None:
                s, c = b.w
                need[s] = max(need.get(s, 0), c)
        for b in writes:
            if b.w is not None:
                s, c = b.w
                need[s] = max(need.get(s, 0), c)
            for s, c in b.readers.items():
                need[s] = max(need.get(s, 0), c)
        waits = {}
        for s, c in need.items():
            if s == stream == "T":
                continue
            waits[s] = c
        self.counts[stream] += 1
        tok = (stream, self.counts[stream])
        for b in writes:
            b.w = tok
            b.readers = {}
        for b in reads:
            b.readers[stream] = self.counts[stream]
        self.steps.append((stream, waits, fn))
        return tok

    def barrier(self):
        snap = dict(self.counts)
        for stream in list(self.counts):
            self.steps.append((stream, dict(snap), ("BARRIER",)))

    def final_waits(self):
        snap = dict(self.counts)
        self.steps.append(("Dp", dict(snap), ("BARRIER",)))

    def realize(self, block, sems):
        per_engine = {"tensor": [], "vector": [], "scalar": [], "gpsimd": [],
                      "sync": []}
        idx = {s: 0 for s in self.counts}
        for stream, waits, fn in self.steps:
            if not isinstance(fn, tuple):
                idx[stream] += 1
            per_engine[self.ENGINE_OF[stream]].append(
                (stream, idx[stream], waits, fn))

        def make_runner(items):
            def runner(eng_obj):
                for stream, myi, waits, fn in items:
                    for s, c in waits.items():
                        if (s == stream and s not in self.DMA_STREAMS
                                and isinstance(fn, tuple)):
                            continue
                        self._emit_wait(eng_obj, sems, stream, s, c)
                    if isinstance(fn, tuple):
                        continue
                    if stream in self.DMA_STREAMS:
                        step = self.DMA_STREAMS[stream]
                        slot = (myi - 1) % NSLOT
                        prev = step * ((myi - 1) // NSLOT)
                        if prev > 0:
                            k = (stream, stream, slot)
                            if self.last_wait.get(k, 0) < prev:
                                eng_obj.wait_ge(sems[(stream, slot)], prev)
                                self.last_wait[k] = prev
                        inst = fn(eng_obj)
                        inst.then_inc(sems[(stream, slot)], step)
                    else:
                        inst = fn(eng_obj)
                        inst.then_inc(sems[stream], 1)
            return runner

        block.tensor(make_runner(per_engine["tensor"]))
        block.vector(make_runner(per_engine["vector"]))
        block.scalar(make_runner(per_engine["scalar"]))
        block.gpsimd(make_runner(per_engine["gpsimd"]))
        block.sync(make_runner(per_engine["sync"]))


class Arena:
    def __init__(self, width):
        self.width = width
        self.off = 0

    def reset(self, at=0):
        self.off = at

    def take(self, n):
        assert self.off + n <= self.width, f"arena overflow {self.off}+{n}"
        o = self.off
        self.off += n
        return o


def hadamard(n):
    h = np.array([[1.0]])
    while h.shape[0] < n:
        h = np.block([[h, h], [h, -h]])
    return h


def host_constants():
    c = {}
    c["cident"] = np.eye(128, dtype=np.float32)
    c["ctril"] = np.triu(np.ones((128, 128), np.float32))
    c["cgmask"] = np.triu(np.ones((128, 128), np.float32))
    c["cgmask2"] = np.concatenate([c["cgmask"], c["cgmask"]], axis=1)
    c["ch128"] = hadamard(128).astype(np.float32)
    hb16 = np.zeros((128, 128), np.float32)
    h16 = hadamard(16) * (2.0 ** -12)
    for b in range(8):
        hb16[b * 16:(b + 1) * 16, b * 16:(b + 1) * 16] = h16
    c["chblk16"] = hb16
    hb64 = np.zeros((128, 128), np.float32)
    h64 = hadamard(64) * (2.0 ** -12)
    for b in range(2):
        hb64[b * 64:(b + 1) * 64, b * 64:(b + 1) * 64] = h64
    c["chblk64"] = hb64
    sel = np.zeros((32, NPAIR * 128), np.float32)
    for j in range(NPAIR):
        sel[2 * j, j * 128:j * 128 + 64] = 1.0
        sel[2 * j + 1, j * 128 + 64:j * 128 + 128] = 1.0
    c["cselpack"] = sel
    sel2 = np.zeros((32, 32 * 64), np.float32)
    for hh in range(32):
        sel2[hh, hh * 64:(hh + 1) * 64] = 1.0
    c["csel2pack"] = sel2
    h8 = hadamard(8)
    m8 = np.zeros((128, 128), np.float32)
    for i in range(128):
        for j in range(128):
            if (i & 15) == (j & 15):
                m8[i, j] = h8[i >> 4, j >> 4]
    c["cm8"] = m8
    c["cones_row"] = np.ones((1, 128), np.float32)
    return c


def per_core_inputs(inputs, cid):
    u = np.asarray(inputs["L0_attn_u"], np.float32)
    b = cid // 4
    t0 = (cid % 4) * TOK
    u_ext = np.zeros((2176, D_MODEL), np.float32)
    lo = max(t0 - 3, 0)
    u_ext[3 - (t0 - lo):3 + TOK] = u[b, lo:t0 + TOK]
    mypos = cid % 4
    msel_t = np.zeros((8, 8), np.float32)
    agmask = np.zeros((8, 1), np.float32)
    for j in range(8):
        if j // 4 == b and (j % 4) < mypos:
            agmask[j, 0] = 1.0
            for m in range(8):
                if m // 4 == b and (j % 4) < (m % 4) < mypos:
                    msel_t[m, j] = 1.0
    return {"u_ext": u_ext, "msel_t": msel_t, "agmask": agmask}


def shared_weight_inputs(inputs):
    W_in = np.asarray(inputs["L0_attn_W_in"], np.float32)
    W_out = np.asarray(inputs["L0_attn_W_out"], np.float32)
    W_up = np.asarray(inputs["L0_ffn_W_up"], np.float32)
    W_gate = np.asarray(inputs["L0_ffn_W_gate"], np.float32)
    W_down = np.asarray(inputs["L0_ffn_W_down"], np.float32)
    d = {}
    W_in_pad = np.zeros((8320, 2048), np.float32)
    W_in_pad[:8224] = W_in
    d["w_in_p"] = np.ascontiguousarray(
        W_in_pad.reshape(65, 128, 16, 128).transpose(3, 0, 2, 1)
        .reshape(128, 65 * 2048))
    d["w_out_t"] = np.ascontiguousarray(W_out.T)
    d["w_up_p"] = np.ascontiguousarray(
        W_up.reshape(64, 128, 16, 128).transpose(3, 0, 2, 1)
        .reshape(128, 64 * 2048))
    d["w_gate_p"] = np.ascontiguousarray(
        W_gate.reshape(64, 128, 16, 128).transpose(3, 0, 2, 1)
        .reshape(128, 64 * 2048))
    d["w_down_f"] = np.ascontiguousarray(
        W_down.reshape(D_MODEL, 128, 64).transpose(1, 2, 0).reshape(128, 64 * D_MODEL))
    d["z_bias_fm"] = np.ascontiguousarray(
        np.asarray(inputs["L0_attn_z_bias"], np.float32).reshape(16, 128).T)
    d["conv_w_fm"] = np.ascontiguousarray(
        np.asarray(inputs["L0_attn_conv_w"], np.float32).reshape(48, 128, 4)
        .transpose(1, 0, 2).reshape(128, 192))
    d["conv_b_fm"] = np.ascontiguousarray(
        np.asarray(inputs["L0_attn_conv_b"], np.float32).reshape(48, 128).T)
    Dv = np.asarray(inputs["L0_attn_D"], np.float32)
    dpair = np.zeros((128, NPAIR), np.float32)
    for j in range(NPAIR):
        dpair[0:64, j] = Dv[2 * j]
        dpair[64:128, j] = Dv[2 * j + 1]
    d["d_pair"] = dpair
    d["rms_w_fm"] = np.ascontiguousarray(
        np.asarray(inputs["L0_attn_rms_w"], np.float32).reshape(16, 128).T)
    d["su_ug_fm"] = np.ascontiguousarray(
        np.asarray(inputs["L0_ffn_SU_ug"], np.float32).reshape(16, 128).T)
    d["su_dp_fm"] = np.ascontiguousarray(
        np.asarray(inputs["L0_ffn_SU_dp"], np.float32).reshape(64, 128).T)
    wsug = np.asarray(inputs["L0_ffn_Wscale_ug"], np.float32)
    d["wsu_up_fm"] = np.ascontiguousarray(wsug[:INTER].reshape(64, 128).T)
    d["wsu_gate_fm"] = np.ascontiguousarray(wsug[INTER:].reshape(64, 128).T)
    d["wsd_row"] = np.ascontiguousarray(
        np.asarray(inputs["L0_ffn_Wscale_dp"], np.float32).reshape(1, 2048))
    return d


def build_program():
    nc = bass.Bass("TRN2", target_bir_lowering=False)
    P = Prog()
    bufs = {}

    def buf(name):
        if name not in bufs:
            bufs[name] = Buf(name)
        return bufs[name]

    def E(stream, fn, reads, writes):
        P.emit(stream, fn, [buf(r) for r in reads], [buf(w) for w in writes])

    def din(name, shape, dt=F32):
        return nc.dram_tensor(name, shape, dt, kind="ExternalInput").ap()

    u_ext = din("u_ext", [2176, D_MODEL])
    w_in_p = din("w_in_p", [128, 65 * 2048])
    w_out_t = din("w_out_t", [D_MODEL, D_MODEL])
    w_up_p = din("w_up_p", [128, 64 * 2048])
    w_gate_p = din("w_gate_p", [128, 64 * 2048])
    w_down_f = din("w_down_f", [128, 64 * D_MODEL])
    z_bias_fm = din("z_bias_fm", [128, 16])
    conv_w_fm = din("conv_w_fm", [128, 192])
    conv_b_fm = din("conv_b_fm", [128, 48])
    d_pair = din("d_pair", [128, NPAIR])
    rms_w_fm = din("rms_w_fm", [128, 16])
    su_ug_fm = din("su_ug_fm", [128, 16])
    su_dp_fm = din("su_dp_fm", [128, 64])
    wsu_up_fm = din("wsu_up_fm", [128, 64])
    wsu_gate_fm = din("wsu_gate_fm", [128, 64])
    wsd_row = din("wsd_row", [1, 2048])
    msel_t = din("msel_t", [8, 8])
    agmask_i = din("agmask", [8, 1])
    cident = din("cident", [128, 128])
    ctril = din("ctril", [128, 128])
    cgmask = din("cgmask", [128, 128])
    cgmask2 = din("cgmask2", [128, 256])
    ch128 = din("ch128", [128, 128])
    chblk16 = din("chblk16", [128, 128])
    chblk64 = din("chblk64", [128, 128])
    cselpack = din("cselpack", [32, NPAIR * 128])
    csel2pack = din("csel2pack", [32, 32 * 64])
    cones_row = din("cones_row", [1, 128])
    cm8 = din("cm8", [128, 128])

    out = nc.dram_tensor("out", [TOK, D_MODEL], F32, kind="ExternalOutput").ap()

    spill_kind = "ExternalOutput" if SPILLS_EXTERNAL else "Internal"

    def dsp(name, shape, dt):
        return nc.dram_tensor(name, shape, dt, kind=spill_kind).ap()

    sp_conv = dsp("sp_conv", [NXT, 128, TOK], BF16)
    sp_xbtm = dsp("sp_xbtm", [32, NCHUNK, 128, 128], BF16)
    sp_z = dsp("sp_z", [16, 128, TOK], BF16)
    sp_y = dsp("sp_y", [NCHUNK, NPAIR, 128, 128], BF16)
    sp_v = dsp("sp_v", [NCHUNK, 64, 4096], BF16)
    sp_x1 = dsp("sp_x1", [NCHUNK, 128, D_MODEL], BF16)
    sp_x0 = dsp("sp_x0", [16, 128, TOK], BF16)
    sp_x2 = nc.dram_tensor("sp_x2", [NSUPER, 64, 128, SUPER], BF16).ap()
    wupB = nc.dram_tensor("wupB", [128, 64 * 2048], BF16).ap()
    wgB = nc.dram_tensor("wgB", [128, 64 * 2048], BF16).ap()
    wdnR = nc.dram_tensor("wdnR", [128, 64 * D_MODEL], BF16).ap()
    agin = nc.dram_tensor("agin", [64, 2056], F32).ap()
    dbg_gm = nc.dram_tensor("dbg_gm", [NPAIR, 128, 256], BF16).ap()
    dbg_xtme = nc.dram_tensor("dbg_xtme", [128, 2048], BF16).ap()
    dbg_eb = nc.dram_tensor("dbg_eb", [128, 2048], BF16).ap()
    dbg_ytmp = nc.dram_tensor("dbg_ytmp", [NPAIR, 128, 128], F32).ap()
    ctxdbg = dict(dbg_gm=dbg_gm, dbg_xtme=dbg_xtme, dbg_eb=dbg_eb, dbg_ytmp=dbg_ytmp)
    agout = nc.dram_tensor("agout", [64 * NCORES, 2056], F32).ap()

    stack = ExitStack()
    sems = {}
    for s in ("T", "V", "S", "P"):
        sems[s] = stack.enter_context(nc.semaphore("sem_" + s))
    for s in ("CC", "Dw", "Ds", "Dp"):
        for r in range(NSLOT):
            sems[(s, r)] = stack.enter_context(nc.semaphore(f"sem_{s}{r}"))

    pf = [stack.enter_context(nc.psum_tensor(f"pf{i}", [128, 512], F32))
          for i in range(4)]
    pf45 = stack.enter_context(nc.psum_tensor("pf45", [128, 1024], F32))
    pf += [pf45[:, 0:512], pf45[:, 512:1024]]
    pf += [stack.enter_context(nc.psum_tensor(f"pf{i}", [128, 512], F32))
           for i in range(6, 8)]
    PFB = [f"PSB{i}" for i in range(8)]
    PFB45 = "PSB45"

    sbt = {}

    def alloc(name, shape, dt):
        sbt[name] = stack.enter_context(nc.sbuf_tensor(name, shape, dt))
        buf(name)
        return sbt[name]

    c_ident_f = alloc("c_ident_f", [128, 128], F32)
    c_tril = alloc("c_tril", [128, 128], F32)
    c_gmask2 = alloc("c_gmask2", [128, 256], F32)
    c_h128f = alloc("c_h128f", [128, 128], F32)
    c_m8p = alloc("c_m8p", [128, 128], BF16)
    c_m8n = alloc("c_m8n", [128, 128], BF16)
    c_hblk16 = alloc("c_hblk16", [128, 128], BF16)
    c_hblk64 = alloc("c_hblk64", [128, 128], BF16)
    c_sel = alloc("c_sel", [32, NPAIR * 128], BF16)
    c_sel2 = alloc("c_sel2", [32, 32 * 64], BF16)
    c_ones_row = alloc("c_ones_row", [1, 128], F32)
    c_ones_colb = alloc("c_ones_colb", [128, 1], BF16)
    p_zbias = alloc("p_zbias", [128, 16], F32)
    p_convw = alloc("p_convw", [128, 192], F32)
    p_convb = alloc("p_convb", [128, 48], F32)
    p_dpair = alloc("p_dpair", [128, NPAIR], F32)
    p_rmsw = alloc("p_rmsw", [128, 16], F32)
    p_suug = alloc("p_suug", [128, 16], F32)
    p_sudp = alloc("p_sudp", [128, 64], F32)
    p_wsuup = alloc("p_wsuup", [128, 64], F32)
    p_wsugt = alloc("p_wsugt", [128, 64], F32)
    p_msel = alloc("p_msel", [8, 8], F32)
    p_agm = alloc("p_agm", [8, 1], F32)
    s_hsu16 = alloc("s_hsu16", [128, 16 * 128], BF16)
    s_hsu64 = alloc("s_hsu64", [128, 64 * 128], BF16)
    a_afm = alloc("a_afm", [32, 2048], F32)
    a_ascr = alloc("a_ascr", [32, 1080], F32)

    AB = alloc("AB", [128, 54016], BF16)
    AFp = alloc("AF", [128, 14348], F32)
    arb = Arena(54016)
    arf = Arena(14348)

    ctx = dict(nc=nc, P=P, E=E, buf=buf, pf=pf, PFB=PFB, arb=arb, arf=arf,
               AB=AB, AF=AFp, sbt=sbt, sems=sems, wsd_row_ap=wsd_row, dbg=ctxdbg, pf45=pf45)

    def ld(q, dst_ap, src_ap, wname, rds=()):
        E(q, lambda e, o=dst_ap, i=src_ap: e.dma_start(out=o, in_=i), rds, [wname])

    ld("Dw", c_ident_f[:, :], cident, "c_ident_f")
    ld("Dw", c_tril[:, :], ctril, "c_tril")
    ld("Dw", c_gmask2[:, :], cgmask2, "c_gmask2")
    ld("Dw", p_zbias[:, :], z_bias_fm, "p_zbias")
    ld("Dw", p_convw[:, :], conv_w_fm, "p_convw")
    ld("Dw", p_convb[:, :], conv_b_fm, "p_convb")
    ld("Dw", p_dpair[:, :], d_pair, "p_dpair")
    ld("Dw", p_rmsw[:, :], rms_w_fm, "p_rmsw")
    ld("Dw", p_suug[:, :], su_ug_fm, "p_suug")
    ld("Dw", p_sudp[:, :], su_dp_fm, "p_sudp")
    ld("Dw", p_msel[:, :], msel_t, "p_msel")
    ld("Dw", p_agm[:, :], agmask_i, "p_agm")
    ld("Dw", c_ones_row[:, :], cones_row, "c_ones_row")
    ld("Dw", c_h128f[:, :], ch128, "h128f")
    o_st = arf.take(256)
    for nm, src, dst, so in (("hblk16", chblk16, c_hblk16, 0),
                             ("hblk64", chblk64, c_hblk64, 128)):
        ld("Dw", AFp[:, o_st + so:o_st + so + 128], src, "gc_" + nm)
        E("V", lambda e, d=dst, so=so: ecopy(e, 
            d[:, :], AFp[:, o_st + so:o_st + so + 128]),
          ["gc_" + nm], [nm + "_b"])
    o_m8 = arf.take(128)
    ld("Dw", AFp[:, o_m8:o_m8 + 128], cm8, "gc_m8")
    E("V", lambda e, o_m8=o_m8: ecopy(e, c_m8p[:, :], AFp[:, o_m8:o_m8 + 128]),
      ["gc_m8"], ["c_m8p"])
    E("V", lambda e: e.tensor_scalar_mul(c_m8n[:, :], c_m8p[:, :], -1.0),
      ["c_m8p"], ["c_m8n"])
    o_sel = arf.take(2048)
    ld("Dw", AFp[0:32, o_sel:o_sel + 2048], cselpack, "gc_sel")
    E("V", lambda e: ecopy(e, c_sel[:, :], AFp[0:32, o_sel:o_sel + 2048]),
      ["gc_sel"], ["c_sel"])
    o_sel2 = arf.take(2048)
    ld("Dw", AFp[0:32, o_sel2:o_sel2 + 2048], csel2pack, "gc_sel2")
    E("V", lambda e: ecopy(e, c_sel2[:, :], AFp[0:32, o_sel2:o_sel2 + 2048]),
      ["gc_sel2"], ["c_sel2"])
    E("V", lambda e: e.memset(c_ones_colb[:, :], 1.0), [], ["c_ones_colb"])
    ld("Dw", p_wsuup[:, :], wsu_up_fm, "wsuup_raw")
    E("V", lambda e: e.tensor_scalar_mul(p_wsuup[:, :], p_wsuup[:, :], 64.0 * CORR16),
      ["wsuup_raw"], ["p_wsuup"])
    ld("Dw", p_wsugt[:, :], wsu_gate_fm, "wsugt_raw")
    E("V", lambda e: e.tensor_scalar_mul(p_wsugt[:, :], p_wsugt[:, :], 64.0 * CORR16),
      ["wsugt_raw"], ["p_wsugt"])
    for c in range(16):
        E("V", lambda e, c=c: e.tensor_scalar_mul(
            s_hsu16[:, c * 128:(c + 1) * 128], c_hblk16[:, :], p_suug[:, c:c + 1]),
          ["hblk16_b", "p_suug"], ["s_hsu16"])
    for uu in range(64):
        E("V", lambda e, u=uu: e.tensor_scalar_mul(
            s_hsu64[:, u * 128:(u + 1) * 128], c_hblk64[:, :], p_sudp[:, u:u + 1]),
          ["hblk64_b", "p_sudp"], ["s_hsu64"])

    P.barrier()
    arb.reset()
    arf.reset()
    o_ufm = arb.take(16 * JPAD)
    o_cvt = arb.take(2 * 2048)
    o_zst = arb.take(2 * 2048)
    o_wpan = arb.take(2 * 2048)
    o_xtmst = arb.take(2 * 2048)
    o_stage = arf.take(2 * JPAD)
    o_acc = arf.take(2 * 2048)
    o_wpanf = arf.take(2 * 2048)

    for c in range(UB_LIM if PHASES >= 1 else 0):
        h = c % 2
        E("Dw", lambda e, c=c, h=h: e.dma_start(
            out=AFp[:, o_stage + h * JPAD:o_stage + h * JPAD + 2048],
            in_=u_ext[c * 128:(c + 1) * 128, :]), [], [f"stg{h}"])
        for g in range(4):
            bank = pf[4 + (g % 2)]
            bname = PFB[4 + (g % 2)]

            def trg(e, h=h, g=g, bank=bank):
                inst = None
                for k in range(4):
                    dk = g * 4 + k
                    inst = e.transpose(
                        bank[:, k * 128:(k + 1) * 128],
                        AFp[:, o_stage + h * JPAD + dk * 128:
                            o_stage + h * JPAD + (dk + 1) * 128],
                        c_ident_f[:, :])
                return inst
            E("T", trg, [f"stg{h}", "c_ident_f"], [bname])

            def evu(e, c=c, g=g, bank=bank):
                w = min(128, 2051 - c * 128)
                inst = None
                for k in range(4):
                    dk = g * 4 + k
                    inst = ecopy(e, 
                        AB[:, o_ufm + dk * JPAD + c * 128:
                           o_ufm + dk * JPAD + c * 128 + w],
                        bank[:, k * 128:k * 128 + w])
                return inst
            E("V" if g % 2 == 0 else "S", evu, [bname], ["a_ufm"])

    def build_fold_steps(o_pf, o_pb, o_ev):
        loads, rests = [], []
        for p in range(64):
            hh = p % 2

            def panload(hh=hh, p=p):
                E("Dw", lambda e: e.dma_start(
                    out=AFp[:, o_pf + hh * 2048:o_pf + (hh + 1) * 2048],
                    in_=w_down_f[:, p * 2048:(p + 1) * 2048]), [], [f"wpf{hh}"])

            def panrest(hh=hh, p=p):
                ob = o_pb
                btok = "fsb0"
                for k in range(4):
                    kb = 4 + (k % 2)
                    E("T", lambda e, k=k, kb=kb: e.matmul(
                        pf[kb][:, :], sbt["c_h128f"][:, :],
                        AFp[:, o_pf + hh * 2048 + k * 512:
                            o_pf + hh * 2048 + (k + 1) * 512],
                        start=True, stop=True),
                      [f"wpf{hh}", "h128f"], [PFB[kb]])
                    E("V" if k % 2 == 0 else "S",
                      lambda e, k=k, kb=kb, ob=ob, btok=btok: ecopy(
                          e, AB[:, ob + k * 512:ob + (k + 1) * 512], pf[kb][:, :]),
                      [PFB[kb]], [btok])
                E("Ds", lambda e, ob=ob: e.dma_start(
                    out=wdnR[:, p * 2048:(p + 1) * 2048],
                    in_=AB[:, ob:ob + 2048]),
                  [btok], [f"fw2_{p}"])
            loads.append(panload)
            rests.append(panrest)
        return loads, rests

    ctx["build_fold_steps"] = build_fold_steps

    o_bev1 = arb.take(2048)
    o_bev2 = arb.take(2048)
    o_bsf = arf.take(1024)
    o_bgf = arf.take(1024)
    fold_loads, fold_rests = build_fold_steps(o_wpanf, o_bev1, o_bev2)

    bounce_steps = []
    bidx = [0]
    for ic in range(128):
        for wsrc, wdst, wi in ((w_up_p, wupB, 0), (w_gate_p, wgB, 1)):
            i = bidx[0]
            bidx[0] += 1
            pipe = i % 2
            dq, ce = ("Ds", "S") if pipe == 0 else ("Dp", "P")
            osf = o_bsf if pipe == 0 else o_bgf
            osb = o_bev2 + pipe * 1024
            ssrc = wsrc[:, ic * 1024:(ic + 1) * 1024]
            sdst = wdst[:, ic * 1024:(ic + 1) * 1024]

            def bstep(ssrc=ssrc, sdst=sdst, dq=dq, ce=ce, osf=osf,
                      osb=osb, pipe=pipe, wi=wi, ic=ic):
                E(dq, lambda e: e.dma_start(
                    out=AFp[:, osf:osf + 1024], in_=ssrc),
                  [], [f"bsf{pipe}"])
                E(ce, lambda e: ecopy(
                    e, AB[:, osb:osb + 1024], AFp[:, osf:osf + 1024]),
                  [f"bsf{pipe}"], [f"bsb{pipe}"])
                E(dq, lambda e: e.dma_start(
                    out=sdst, in_=AB[:, osb:osb + 1024]),
                  [f"bsb{pipe}"], [f"fw{wi}_{ic}"])
            bounce_steps.append(bstep)

    def drain_bounce(n):
        for _ in range(n):
            if bounce_steps:
                bounce_steps.pop(0)()
    fold_state = [0]

    def drain_fold(n):
        for _ in range(n):
            i = fold_state[0]
            if i > 64:
                return
            if i < 64:
                fold_loads[i]()
            if i >= 1:
                fold_rests[i - 1]()
            fold_state[0] = i + 1
    ctx["drain_fold"] = drain_fold
    ctx["drain_bounce"] = drain_bounce

    tblocks = [(0, 512), (512, 512), (1024, 512), (1536, 512), (2048, 3)]

    def emit_proj_head(cc):
        h = cc % 2
        src = w_in_p[:, cc * 2048:(cc + 1) * 2048]
        E("Dw", lambda e, h=h, src=src: e.dma_start(
            out=AFp[:, o_wpanf + h * 2048:o_wpanf + h * 2048 + 2048], in_=src),
          [], [f"wpf{h}"])
        E("V",
          lambda e, h=h: ecopy(e,
              AB[:, o_wpan + h * 2048:o_wpan + h * 2048 + 2048],
              AFp[:, o_wpanf + h * 2048:o_wpanf + h * 2048 + 2048]),
          [f"wpf{h}"], [f"wpan{h}"])
        drain_fold(1)
        drain_bounce(4)

    def emit_proj(cc):
        h = cc % 2
        for tb, (j0, N) in enumerate(tblocks):
            bank = pf[tb % 2]
            bname = PFB[tb % 2]

            def do_mm(e, h=h, j0=j0, N=N, bank=bank):
                inst = None
                for dk in range(16):
                    inst = e.matmul(
                        bank[0:128, 0:N],
                        AB[:, o_wpan + h * 2048 + dk * 128:
                           o_wpan + h * 2048 + (dk + 1) * 128],
                        AB[:, o_ufm + dk * JPAD + j0:o_ufm + dk * JPAD + j0 + N],
                        start=(dk == 0), stop=(dk == 15))
                return inst
            E("T", do_mm, [f"wpan{h}", "a_ufm"], [bname])
            if cc < 48:
                E("S",
                  lambda e, h=h, j0=j0, N=N, bank=bank: ecopy(e,
                      AFp[:, o_stage + h * JPAD + j0:o_stage + h * JPAD + j0 + N],
                      bank[0:128, 0:N]),
                  [bname], [f"stg{h}"])
            else:
                lo = max(j0, 3)
                hi = j0 + N
                if hi <= 3:
                    continue
                if cc < 64:
                    zc = cc - 48
                    E("S", lambda e, h=h, zc=zc, lo=lo, hi=hi, j0=j0, bank=bank:
                      e.activation(
                          AB[:, o_zst + h * 2048 + lo - 3:o_zst + h * 2048 + hi - 3],
                          bank[0:128, lo - j0:hi - j0], AF.Silu,
                          bias=p_zbias[:, zc:zc + 1], scale=1.0),
                      [bname, "p_zbias"], [f"zst{h}"])
                else:
                    n = hi - lo
                    sA = bank[0:32, lo - j0:hi - j0]
                    E("S", lambda e, sA=sA, n=n: e.activation(
                        a_ascr[0:32, 0:n], sA, AF.Relu), [bname], ["Ar"])
                    E("S", lambda e, sA=sA, n=n: e.activation(
                        a_ascr[0:32, 550:550 + n], sA, AF.Abs), [bname], ["Aa"])
                    E("S", lambda e, n=n: e.activation(
                        a_ascr[0:32, 550:550 + n], a_ascr[0:32, 550:550 + n],
                        AF.Exp, scale=-1.0), ["Aa"], ["Aa"])
                    E("S", lambda e, n=n: e.activation(
                        a_ascr[0:32, 550:550 + n], a_ascr[0:32, 550:550 + n],
                        AF.Ln, bias=1.0), ["Aa"], ["Aa"])
                    E("V", lambda e, lo=lo, hi=hi, n=n: e.scalar_tensor_tensor(
                        a_afm[0:32, lo - 3:hi - 3], a_ascr[0:32, 0:n], -1.0,
                        a_ascr[0:32, 550:550 + n], ALU.mult, ALU.subtract),
                      ["Ar", "Aa"], ["a_afm"])
    def emit_conv_compute(cc):
        h = cc % 2
        if cc < 48:
            sg = o_stage + h * JPAD
            aA = AFp[:, o_acc + h * 2048:o_acc + (h + 1) * 2048]
            aB = AFp[:, o_acc + (1 - h) * 2048:o_acc + (2 - h) * 2048]
            E("V", lambda e, sg=sg, aA=aA, cc=cc: e.tensor_scalar(
                aA, AFp[:, sg:sg + 2048], p_convw[:, cc * 4:cc * 4 + 1],
                p_convb[:, cc:cc + 1], ALU.mult, ALU.add),
              [f"stg{h}", "p_convw", "p_convb"], [f"acch{h}"])
            E("V", lambda e, sg=sg, aA=aA, aB=aB, cc=cc: e.scalar_tensor_tensor(
                aB, AFp[:, sg + 1:sg + 2049], p_convw[:, cc * 4 + 1:cc * 4 + 2],
                aA, ALU.mult, ALU.add),
              [f"stg{h}", f"acch{h}", "p_convw"], [f"acch{1-h}"])
            E("V", lambda e, sg=sg, aA=aA, aB=aB, cc=cc: e.scalar_tensor_tensor(
                aA, AFp[:, sg + 2:sg + 2050], p_convw[:, cc * 4 + 2:cc * 4 + 3],
                aB, ALU.mult, ALU.add),
              [f"stg{h}", f"acch{1-h}", "p_convw"], [f"acch{h}"])
            E("V", lambda e, sg=sg, aA=aA, aB=aB, cc=cc: e.scalar_tensor_tensor(
                aB, AFp[:, sg + 3:sg + 2051], p_convw[:, cc * 4 + 3:cc * 4 + 4],
                aA, ALU.mult, ALU.add),
              [f"stg{h}", f"acch{h}", "p_convw"], [f"acch{1-h}"])

    def emit_conv_tail(cc):
        h = cc % 2
        if cc < 48:
            aB = AFp[:, o_acc + (1 - h) * 2048:o_acc + (2 - h) * 2048]
            E("S", lambda e, aB=aB, h=h: e.activation(
                AB[:, o_cvt + h * 2048:o_cvt + (h + 1) * 2048], aB, AF.Silu),
              [f"acch{1-h}"], [f"cvo{h}"])
            E("Dp", lambda e, cc=cc, h=h: e.dma_start(
                out=sp_conv[cc], in_=AB[:, o_cvt + h * 2048:o_cvt + (h + 1) * 2048]),
              [f"cvo{h}"], [f"spc{cc}"])
        elif cc < 64:
            E("Ds", lambda e, cc=cc, h=h: e.dma_start(
                out=sp_z[cc - 48], in_=AB[:, o_zst + h * 2048:o_zst + (h + 1) * 2048]),
              [f"zst{h}"], [f"spz{cc-48}"])

    def emit_conv_trans(cc):
        h = cc % 2
        if cc < 32:
            aB = AFp[:, o_acc + (1 - h) * 2048:o_acc + (2 - h) * 2048]
            if True:
                for g in range(4):
                    bank2 = pf[4 + (g % 2)]
                    bname2 = PFB[4 + (g % 2)]

                    def trx(e, aB=aB, g=g, bank2=bank2):
                        inst = None
                        for k in range(4):
                            tcx = g * 4 + k
                            inst = e.transpose(
                                bank2[:, k * 128:(k + 1) * 128],
                                aB[:, tcx * 128:(tcx + 1) * 128],
                                c_ident_f[:, :])
                        return inst
                    E("T", trx, [f"acch{1-h}", "c_ident_f"], [bname2])
                    E("S", lambda e, h=h, g=g, bank2=bank2: e.activation(
                        AB[:, o_xtmst + h * 2048 + g * 512:
                           o_xtmst + h * 2048 + (g + 1) * 512],
                        bank2[:, 0:512], AF.Silu), [bname2], [f"xts{h}"])
                E("Dp", lambda e, cc=cc, h=h: e.dma_start(
                    out=sp_xbtm[cc].rearrange("c p t -> p c t"),
                    in_=AB[:, o_xtmst + h * 2048:o_xtmst + (h + 1) * 2048]),
                  [f"xts{h}"], [f"spt{cc}"])

    if PHASES >= 2:
        emit_proj_head(0)
    for cc in range(65 if PHASES >= 2 else 0):
        if cc >= 1:
            emit_conv_compute(cc - 1)
        if cc + 1 < 65:
            emit_proj_head(cc + 1)
        emit_proj(cc)
        if cc >= 1:
            emit_conv_tail(cc - 1)
            emit_conv_trans(cc - 1)

    drain_fold(1000)
    drain_bounce(1000)

    if PHASES >= 4:
        emit_ssd(ctx, sp_conv, sp_xbtm, sp_y, sp_v, agin)
    if PHASES >= 5:
        emit_combine(ctx, agin, agout, w_out_t)
    if PHASES >= 6:
        emit_p3(ctx, w_out_t, u_ext, sp_y, sp_v, sp_z, sp_x1, sp_x0)
    if PHASES >= 7:
        emit_p4(ctx, wupB, wgB, wdnR, sp_x2, sp_x0, sp_x1, out)
    else:
        P.barrier()
        E("V", lambda e: e.memset(AFp[:, 0:2048], 0.0), [], ["stubz"])
        for c in range(16):
            E("Dp", lambda e, c=c: e.dma_start(
                out=out[c * 128:(c + 1) * 128, :], in_=AFp[:, 0:2048]),
              ["stubz"], [f"outc{c}"])

    P.final_waits()
    with nc.Block() as block:
        P.realize(block, sems)
    stack.close()
    return nc


def emit_ssd(ctx, sp_conv, sp_xbtm, sp_y, sp_v, agin):
    P, E, pf, PFB = ctx["P"], ctx["E"], ctx["pf"], ctx["PFB"]
    arb, arf, AB, AFp, sbt = ctx["arb"], ctx["arf"], ctx["AB"], ctx["AF"], ctx["sbt"]
    c_tril = sbt["c_tril"]
    c_ident_f = sbt["c_ident_f"]
    c_sel = sbt["c_sel"]
    c_sel2 = sbt["c_sel2"]
    a_afm = sbt["a_afm"]
    a_ascr = sbt["a_ascr"]
    p_dpair = sbt["p_dpair"]
    pf45 = ctx["pf45"]

    P.barrier()
    arb.reset()
    arf.reset()
    o_slx = arb.take(2 * 2048)
    o_bsl = arb.take(2 * 4096)
    o_csl = arb.take(2 * 4096)
    o_xtm = arb.take(2 * 2048)
    o_btm = arb.take(2 * 2048)
    o_xtme = arb.take(2 * 2048)
    o_gm = arb.take(2 * 256)
    o_eb = arb.take(2 * 2048)
    o_ebghm = arb.take(4096)
    o_sprefs = arb.take(2048)
    o_yst = arb.take(2 * 2048)
    o_vst = arb.take(4096)
    o_ehsb = arb.take(256)
    o_ex3b = arb.take(16)
    o_einvb = arb.take(2 * 128)
    o_wbs = arb.take(2 * 64)
    o_wsc = arb.take(2 * 128)
    o_ones64 = arb.take(64)
    o_srun = arf.take(2048)
    o_schunk = arf.take(2048)
    o_tsc2 = arf.take(2048)
    o_tscan = arf.take(64)
    o_ytmp = arf.take(512)
    o_acol = arf.take(128)
    o_scolh = arf.take(2 * 96)

    E("V", lambda e: e.memset(AB[64:128, o_bsl:o_bsl + 2 * 4096], 0.0),
      [], ["bslpad"])
    E("V", lambda e: e.memset(AB[64:128, o_csl:o_csl + 2 * 4096], 0.0),
      [], ["cslpad"])
    E("V", lambda e: e.memset(AB[64:128, o_sprefs:o_sprefs + 2048], 0.0),
      [], ["sprefspad"])
    E("V", lambda e: e.memset(AFp[0:64, o_srun:o_srun + 2048], 0.0), [], ["srun"])
    E("V", lambda e: e.memset(a_ascr[0:32, 512:513], 0.0), [], ["RD"])
    E("V", lambda e: e.memset(AB[0:32, o_ones64:o_ones64 + 64], 1.0),
      [], ["ones64"])

    def emit_headA(c):
        h = c % 2
        slb0 = o_slx + h * 2048
        E("Dw", lambda e, c=c, slb0=slb0: e.dma_start(
            out=AB[:, slb0:slb0 + 2048],
            in_=sp_conv[0:16, :, c * 128:(c + 1) * 128].rearrange(
                "a p t -> p a t")),
          [f"spc{cc}" for cc in range(16)], [f"sl{h}"])
        E("Dw", lambda e, c=c, h=h: e.dma_start(
            out=AB[0:64, o_bsl + h * 4096:o_bsl + (h + 1) * 4096],
            in_=sp_conv[16:32, :, c * 128:(c + 1) * 128].rearrange(
                "j (e n) t -> n (j e) t", e=2, n=64)),
          [f"spc{cc}" for cc in range(16, 32)], [f"bsl{h}"])
        E("Dw", lambda e, c=c, h=h: e.dma_start(
            out=AB[0:64, o_csl + h * 4096:o_csl + (h + 1) * 4096],
            in_=sp_conv[32:48, :, c * 128:(c + 1) * 128].rearrange(
                "j (e n) t -> n (j e) t", e=2, n=64)),
          [f"spc{cc}" for cc in range(32, 48)], [f"csl{h}"])
        E("Dw", lambda e, c=c, h=h: e.dma_start(
            out=AB[:, o_xtm + h * 2048:o_xtm + (h + 1) * 2048],
            in_=sp_xbtm[0:16, c].rearrange("a p t -> p a t")),
          [f"spt{cc}" for cc in range(16)], [f"xtm{h}"])
        E("Dw", lambda e, c=c, h=h: e.dma_start(
            out=AB[:, o_btm + h * 2048:o_btm + (h + 1) * 2048],
            in_=sp_xbtm[16:32, c].rearrange("a p t -> p a t")),
          [f"spt{cc+16}" for cc in range(16)], [f"btm{h}"])

        E("T", lambda e, c=c: e.transpose(
            pf[6][:, 0:32], a_afm[0:32, c * 128:(c + 1) * 128],
            c_ident_f[0:32, 0:32]),
          ["a_afm", "c_ident_f"], [PFB[6]])
        E("V", lambda e: ecopy(e, AFp[:, o_acol:o_acol + 32], pf[6][:, 0:32]),
          [PFB[6]], ["At"])
        E("T", lambda e: e.matmul(pf[4][:, 384:416], c_tril[:, :],
                                  AFp[:, o_acol:o_acol + 32], start=True, stop=True),
          ["At", "c_tril"], [PFB[4]])
        E("V", lambda e: ecopy(e, AFp[:, o_acol + 32:o_acol + 64],
                               pf[4][:, 384:416]), [PFB[4]], ["CS"])
        E("T", lambda e: e.transpose(pf[5][0:32, 384:512],
                                     AFp[:, o_acol + 32:o_acol + 64],
                                     c_ident_f[:, :]),
          ["CS", "c_ident_f"], [PFB[5]])
        E("V", lambda e: ecopy(e, a_ascr[0:32, 0:128], pf[5][0:32, 384:512]),
          [PFB[5]], ["CS_hs"])
        E("V", lambda e: e.tensor_scalar(
            a_ascr[0:32, 128:256], a_ascr[0:32, 0:128], a_ascr[0:32, 64:65], None,
            ALU.subtract), ["CS_hs"], ["CSm"])
        E("S", lambda e: e.activation(a_ascr[0:32, 256:384], a_ascr[0:32, 128:256],
                                      AF.Exp), ["CSm"], ["E_hs"])
        E("S", lambda e: e.activation(a_ascr[0:32, 384:512], a_ascr[0:32, 128:256],
                                      AF.Exp, scale=-1.0), ["CSm"], ["Einv_hs"])
        E("V", lambda e: ecopy(e, AB[0:32, o_ehsb:o_ehsb + 128],
                               a_ascr[0:32, 256:384]), ["E_hs"], ["E_hsb"])
        E("V", lambda e: e.tensor_tensor(a_ascr[0:32, 524:525],
                                         a_ascr[0:32, 64:65],
                                         a_ascr[0:32, 512:513], ALU.add),
          ["CS_hs", "RD"], ["mRD"])
        E("S", lambda e: e.activation(a_ascr[0:32, 525:526], a_ascr[0:32, 524:525],
                                      AF.Exp), ["mRD"], ["expmRD"])
        E("V", lambda e: e.tensor_scalar(
            AB[0:32, o_ehsb + 128:o_ehsb + 256], a_ascr[0:32, 256:384],
            a_ascr[0:32, 525:526], None, ALU.mult),
          ["E_hs", "expmRD"], ["Eg_hsb"])
        E("V", lambda e: e.tensor_tensor(a_ascr[0:32, 512:513],
                                         a_ascr[0:32, 512:513],
                                         a_ascr[0:32, 127:128], ALU.add),
          ["CS_hs", "RD"], ["RD"])
        E("V", lambda e: ecopy(e, a_ascr[0:32, 520:521],
                               a_ascr[0:32, 127:128]), ["CS_hs"], ["rhs3"])
        E("V", lambda e: e.tensor_scalar(
            a_ascr[0:32, 521:522], a_ascr[0:32, 127:128], a_ascr[0:32, 64:65],
            None, ALU.subtract), ["CS_hs"], ["rhs3"])
        E("V", lambda e: ecopy(e, a_ascr[0:32, 522:523],
                               a_ascr[0:32, 64:65]), ["CS_hs"], ["rhs3"])
        E("S", lambda e: e.activation(a_ascr[0:32, 530:533], a_ascr[0:32, 520:523],
                                      AF.Exp), ["rhs3"], ["exp3"])
        E("V", lambda e, h=h: ecopy(e, AB[0:32, o_einvb + h * 128:
                                          o_einvb + h * 128 + 128],
                                    a_ascr[0:32, 384:512]),
          ["Einv_hs"], [f"einvb{h}"])
        E("V", lambda e, h=h: e.tensor_scalar(
            AB[0:32, o_wbs + h * 64:o_wbs + (h + 1) * 64],
            AB[0:32, o_ones64:o_ones64 + 64],
            a_ascr[0:32, 532:533], None, ALU.mult),
          ["ones64", "exp3"], [f"wbs{h}"])
        E("V", lambda e, h=h: e.tensor_scalar(
            AB[0:32, o_wsc + h * 128:o_wsc + h * 128 + 64],
            AB[0:32, o_ones64:o_ones64 + 64],
            a_ascr[0:32, 530:531], None, ALU.mult),
          ["ones64", "exp3"], [f"wsc0{h}"])
        E("V", lambda e, h=h: e.tensor_scalar(
            AB[0:32, o_wsc + h * 128 + 64:o_wsc + h * 128 + 128],
            AB[0:32, o_ones64:o_ones64 + 64],
            a_ascr[0:32, 531:532], None, ALU.mult),
          ["ones64", "exp3"], [f"wsc1{h}"])

    def emit_headB(c):
        h = c % 2
        for g in range(4):
            bank = pf[6 + (g % 2)]
            bname = PFB[6 + (g % 2)]

            def ebmm(e, g=g, bank=bank):
                inst = None
                for q in range(4):
                    jj = g * 4 + q
                    inst = e.matmul(bank[:, q * 128:(q + 1) * 128],
                                    c_sel[:, jj * 128:(jj + 1) * 128],
                                    AB[0:32, o_ehsb:o_ehsb + 128],
                                    start=True, stop=True)
                return inst
            E("T", ebmm, ["E_hsb", "c_sel"], [bname])
            E("V", lambda e, g=g, h=h, bank=bank: ecopy(e,
                AB[:, o_eb + h * 2048 + g * 512:o_eb + h * 2048 + (g + 1) * 512],
                bank[:, 0:512]),
              [bname], [f"eb{h}"])
        for g in range(8):
            bank = pf[6 + (g % 2)]
            bname = PFB[6 + (g % 2)]

            def ebgmm(e, g=g, bank=bank):
                inst = None
                for q in range(4):
                    hh = g * 4 + q
                    inst = e.matmul(bank[0:64, q * 128:(q + 1) * 128],
                                    c_sel2[:, hh * 64:(hh + 1) * 64],
                                    AB[0:32, o_ehsb + 128:o_ehsb + 256],
                                    start=True, stop=True)
                return inst
            E("T", ebgmm, ["Eg_hsb", "c_sel2"], [bname])
            E("S", lambda e, g=g, bank=bank: e.copy(
                AB[0:64, o_ebghm + g * 512:o_ebghm + (g + 1) * 512],
                bank[0:64, 0:512]),
              [bname], ["ebghm"])
        for g in range(4):
            bank = pf[6 + (g % 2)]
            bname = PFB[6 + (g % 2)]
            E("T", lambda e, g=g, h=h, bank=bank: e.matmul(
                bank[:, 0:512],
                AB[0:32, o_einvb + h * 128:o_einvb + h * 128 + 128],
                c_sel[:, g * 512:(g + 1) * 512], start=True, stop=True),
              [f"einvb{h}", "c_sel"], [bname])
            E("V", lambda e, g=g, h=h, bank=bank: e.tensor_tensor(
                AB[:, o_xtme + h * 2048 + g * 512:
                   o_xtme + h * 2048 + (g + 1) * 512],
                AB[:, o_xtm + h * 2048 + g * 512:
                   o_xtm + h * 2048 + (g + 1) * 512],
                bank[:, 0:512], ALU.mult),
              [f"xtm{h}", bname], [f"xtme{h}"])
        E("P", lambda e, h=h: e.tensor_tensor(
            AB[0:64, o_vst:o_vst + 4096],
            AB[0:64, o_csl + h * 4096:o_csl + (h + 1) * 4096],
            AB[0:64, o_ebghm:o_ebghm + 4096], ALU.mult),
          [f"csl{h}", "ebghm"], ["vst"])
        E("Dw", lambda e, c=c: e.dma_start(
            out=sp_v[c], in_=AB[0:64, o_vst:o_vst + 4096]),
          ["vst"], [f"spv{c}"])

    def emit_sprefs(c):
        h = c % 2
        for g in range(4):
            bank = pf[4 + (g % 2)]
            bname = PFB[4 + (g % 2)]
            E("T", lambda e, g=g, h=h, bank=bank: e.matmul(
                bank[0:64, 0:512],
                AB[0:32, o_wbs + h * 64:o_wbs + (h + 1) * 64],
                c_sel2[:, g * 512:(g + 1) * 512], start=True, stop=True),
              [f"wbs{h}", "c_sel2"], [bname])
            E("V", lambda e, g=g, bank=bank: e.tensor_tensor(
                AB[0:64, o_sprefs + g * 512:o_sprefs + (g + 1) * 512],
                AFp[0:64, o_srun + g * 512:o_srun + (g + 1) * 512],
                bank[0:64, 0:512], ALU.mult),
              ["srun", bname], ["sprefs"])

    def emit_pairs(c, j0, j1):
        h = c % 2
        slb0 = o_slx + h * 2048

        def emit_gp(jj):
            gbank = pf[0 + (jj % 2)]
            gbname = PFB[0 + (jj % 2)]

            def gmm(e, jj=jj, h=h, gbank=gbank):
                inst = None
                for e_ in range(2):
                    hh = 2 * jj + e_
                    inst = e.matmul(
                        gbank[:, e_ * 128:(e_ + 1) * 128],
                        AB[:, o_bsl + h * 4096 + hh * 128:
                           o_bsl + h * 4096 + (hh + 1) * 128],
                        AB[:, o_csl + h * 4096 + hh * 128:
                           o_csl + h * 4096 + (hh + 1) * 128],
                        start=True, stop=True)
                return inst
            E("T", gmm, [f"bsl{h}", f"csl{h}", "bslpad", "cslpad"], [gbname])

        def emit_fin(jj):
            gbank = pf[0 + (jj % 2)]
            gbname = PFB[0 + (jj % 2)]
            g = jj // 4
            q = jj % 4
            ybank = pf[2 + (g % 2)]
            ybname = PFB[2 + (g % 2)]
            gmo = o_gm + (jj % 2) * 256
            E("V", lambda e, gmo=gmo, gbank=gbank: e.tensor_tensor(
                AB[:, gmo:gmo + 256], gbank[:, 0:256],
                sbt["c_gmask2"][:, :], ALU.mult),
              [gbname, "c_gmask2"], [f"gm{jj%2}"])

            def ymm(e, jj=jj, h=h, ybank=ybank, q=q, gmo=gmo):
                inst = None
                for e_ in range(2):
                    hh = 2 * jj + e_
                    xe = AB[:, o_xtme + h * 2048 + jj * 128 + e_ * 64:
                            o_xtme + h * 2048 + jj * 128 + e_ * 64 + 64]
                    inst = e.matmul(
                        ybank[e_ * 64:(e_ + 1) * 64, q * 128:(q + 1) * 128], xe,
                        AB[:, gmo + e_ * 128:gmo + (e_ + 1) * 128],
                        start=True, stop=False)
                    sp = AB[:, o_sprefs + hh * 64:o_sprefs + (hh + 1) * 64]
                    cc_ = AB[:, o_csl + h * 4096 + hh * 128:
                             o_csl + h * 4096 + (hh + 1) * 128]
                    inst = e.matmul(
                        ybank[e_ * 64:(e_ + 1) * 64, q * 128:(q + 1) * 128], sp,
                        cc_, start=False, stop=True)
                return inst
            E("T", ymm, [f"xtme{h}", f"gm{jj%2}", "sprefs", "sprefspad", f"csl{h}",
                         "cslpad"], [ybname])

            def smm(e, jj=jj, h=h):
                inst = None
                for e_ in range(2):
                    bt = AB[:, o_btm + h * 2048 + jj * 128 + e_ * 64:
                            o_btm + h * 2048 + jj * 128 + e_ * 64 + 64]
                    xe = AB[:, o_xtme + h * 2048 + jj * 128 + e_ * 64:
                            o_xtme + h * 2048 + jj * 128 + e_ * 64 + 64]
                    inst = e.matmul(
                        pf[0 + (jj % 2)][0:64, 256 + e_ * 64:256 + (e_ + 1) * 64],
                        bt, xe, start=True, stop=True)
                return inst
            E("T", smm, [f"btm{h}", f"xtme{h}"], [PFB[0 + (jj % 2)]])
            E("S", lambda e, jj=jj: ecopy(e,
                AFp[0:64, o_schunk + jj * 128:o_schunk + (jj + 1) * 128],
                pf[0 + (jj % 2)][0:64, 256:384]),
              [PFB[0 + (jj % 2)]], ["schunk"])
            if q == 3:
                E("V", lambda e, g=g, h=h, ybank=ybank: e.tensor_tensor(
                    AFp[:, o_ytmp:o_ytmp + 512], ybank[:, 0:512],
                    AB[:, o_eb + h * 2048 + g * 512:o_eb + h * 2048 + (g + 1) * 512],
                    ALU.mult),
                  [ybname, f"eb{h}"], ["ytmp"])
                for q2 in range(4):
                    jj2 = g * 4 + q2
                    xpair = AB[:, slb0 + jj2 * 128:slb0 + (jj2 + 1) * 128]
                    E("V", lambda e, jj2=jj2, h=h, xpair=xpair, q2=q2:
                      e.scalar_tensor_tensor(
                        AB[:, o_yst + h * 2048 + jj2 * 128:
                           o_yst + h * 2048 + (jj2 + 1) * 128],
                        xpair, p_dpair[:, jj2:jj2 + 1],
                        AFp[:, o_ytmp + q2 * 128:o_ytmp + (q2 + 1) * 128],
                        ALU.mult, ALU.add),
                      [f"sl{h}", "p_dpair", "ytmp"], [f"yst{h}"])

        for jj in range(j0, j1):
            emit_gp(jj)
            if jj >= 1:
                emit_fin(jj - 1)
        if j1 == NPAIR:
            emit_fin(NPAIR - 1)
            E("Dw", lambda e, c=c, h=h: e.dma_start(
                out=sp_y[c].rearrange("j p t -> p j t"),
                in_=AB[:, o_yst + h * 2048:o_yst + (h + 1) * 2048]),
              [f"yst{h}"], [f"spy{c}"])
            for g in range(4):
                bank = pf[4 + g]
                bname = PFB[4 + g]
                E("T", lambda e, g=g, h=h, bank=bank: e.matmul(
                    bank[0:64, 0:512],
                    AB[0:32, o_wsc + h * 128:o_wsc + h * 128 + 64],
                    c_sel2[:, g * 512:(g + 1) * 512], start=True, stop=True),
                  [f"wsc0{h}", "c_sel2"], [bname])
                E("T", lambda e, g=g, h=h, bank=bank: e.matmul(
                    bank[64:128, 0:512],
                    AB[0:32, o_wsc + h * 128 + 64:o_wsc + h * 128 + 128],
                    c_sel2[:, g * 512:(g + 1) * 512], start=True, stop=True),
                  [f"wsc1{h}", "c_sel2"], [bname])
                E("V", lambda e, g=g, bank=bank: e.tensor_tensor(
                    AFp[0:64, o_srun + g * 512:o_srun + (g + 1) * 512],
                    AFp[0:64, o_srun + g * 512:o_srun + (g + 1) * 512],
                    bank[0:64, 0:512], ALU.mult),
                  ["srun", bname], ["srun"])
                E("V", lambda e, g=g, bank=bank: e.tensor_tensor(
                    AFp[0:64, o_tsc2 + g * 512:o_tsc2 + (g + 1) * 512],
                    AFp[0:64, o_schunk + g * 512:o_schunk + (g + 1) * 512],
                    bank[64:128, 0:512], ALU.mult),
                  ["schunk", bname], ["tsc2"])
                E("V", lambda e, g=g: e.tensor_tensor(
                    AFp[0:64, o_srun + g * 512:o_srun + (g + 1) * 512],
                    AFp[0:64, o_srun + g * 512:o_srun + (g + 1) * 512],
                    AFp[0:64, o_tsc2 + g * 512:o_tsc2 + (g + 1) * 512],
                    ALU.add), ["srun", "tsc2"], ["srun"])

    drain_bounce = ctx["drain_bounce"]
    if SSD_LIM > 0:
        emit_headA(0)
        emit_headB(0)
    for c in range(SSD_LIM):
        emit_sprefs(c)
        if c + 1 < SSD_LIM:
            emit_headA(c + 1)
        emit_pairs(c, 0, 8)
        if c + 1 < SSD_LIM:
            emit_headB(c + 1)
        emit_pairs(c, 8, NPAIR)

    E("V", lambda e: e.memset(AFp[0:64, o_tscan:o_tscan + 16], 0.0),
      ["tscan"], ["zpad"])
    E("Ds", lambda e: e.dma_start(out=agin[0:64, 2048:2056],
                                  in_=AFp[0:64, o_tscan:o_tscan + 8]),
      ["zpad"], ["agin"])
    E("Ds", lambda e: e.dma_start(out=agin[0:64, 0:2048],
                                  in_=AFp[0:64, o_srun:o_srun + 2048]),
      ["srun"], ["agin"])
    E("Ds", lambda e: e.dma_start(out=agin[0:32, 2048:2049],
                                  in_=a_ascr[0:32, 512:513],
                                  allow_slow_non_contiguous=True),
      ["RD"], ["agin"])

def emit_combine(ctx, agin, agout, woutbf):
    P, E, pf, PFB = ctx["P"], ctx["E"], ctx["pf"], ctx["PFB"]
    arb, arf, AB, AFp, sbt = ctx["arb"], ctx["arf"], ctx["AB"], ctx["AF"], ctx["sbt"]
    c_sel2 = sbt["c_sel2"]
    c_ident_f = sbt["c_ident_f"]
    p_msel = sbt["p_msel"]
    p_agm = sbt["p_agm"]
    a_ascr = sbt["a_ascr"]

    P.barrier()
    arb.reset()
    arf.reset()
    o_sinb = arb.take(2048 + 64)
    arb.take(64)
    o_woutc = arb.take(16 * 2048)
    o_ones64c = arb.take(64)
    o_wbc = arb.take(64)
    o_sj = arf.take(3 * 2048)
    o_tmp = arf.take(2048)
    o_sinA = arf.take(2048)
    o_wstg = arf.take(2 * 2048)
    assert o_woutc == 2176

    E("CC", lambda e: e.collective_compute(
        "AllGather", ALU.bypass, replica_groups=[list(range(NCORES))],
        ins=[agin.opt()], outs=[agout.opt()]),
      ["agin"], ["agout"])
    for ec in range(16):
        h2 = ec % 2
        E("Dw", lambda e, ec=ec, h2=h2: e.dma_start(
            out=AFp[:, o_wstg + h2 * 2048:o_wstg + (h2 + 1) * 2048],
            in_=woutbf[ec * 128:(ec + 1) * 128, :]), [], [f"wst{h2}"])
        E("V" if ec % 2 == 0 else "S",
          lambda e, ec=ec, h2=h2: ecopy(e,
              AB[:, o_woutc + ec * 2048:o_woutc + (ec + 1) * 2048],
              AFp[:, o_wstg + h2 * 2048:o_wstg + (h2 + 1) * 2048]),
          [f"wst{h2}"], ["woutres"])
    src = bass.AP(agout.tensor, 2048, [[64 * 2056, 8], [2056, 32]])
    E("Ds", lambda e, src=src: e.dma_start(
        out=a_ascr[0:8, 600:632], in_=src, allow_slow_non_contiguous=True),
      ["agout"], ["RDallT"])
    E("T", lambda e: e.matmul(pf[6][0:8, 0:32], p_msel[:, :], a_ascr[0:8, 600:632],
                              start=True, stop=True),
      ["RDallT", "p_msel"], [PFB[6]])
    E("S", lambda e: e.activation(a_ascr[0:8, 640:672], pf[6][0:8, 0:32], AF.Exp),
      [PFB[6]], ["expw"])
    E("V", lambda e: e.tensor_scalar(a_ascr[0:8, 640:672], a_ascr[0:8, 640:672],
                                     p_agm[:, 0:1], None, ALU.mult),
      ["expw", "p_agm"], ["expw"])
    E("T", lambda e: e.transpose(pf[6][0:32, 32:40], a_ascr[0:8, 640:672],
                                 c_ident_f[0:8, 0:8]),
      ["expw", "c_ident_f"], [PFB[6]])
    E("V", lambda e: ecopy(e, a_ascr[0:32, 700:708], pf[6][0:32, 32:40]),
      [PFB[6]], ["wT"])
    E("V", lambda e: e.memset(AB[0:32, o_ones64c:o_ones64c + 64], 1.0),
      [], ["ones64c"])

    def ldj(j):
        sl = j % 3
        E("Ds", lambda e, j=j, sl=sl: e.dma_start(
            out=AFp[0:64, o_sj + sl * 2048:o_sj + (sl + 1) * 2048],
            in_=agout[j * 64:(j + 1) * 64, 0:2048]),
          ["agout"], [f"sj{sl}"])
    for j in range(3):
        ldj(j)
    for j in range(8):
        sl = j % 3
        if j + 3 < 8:
            ldj(j + 3)
        E("V", lambda e, j=j: e.tensor_scalar(
            AB[0:32, o_wbc:o_wbc + 64], AB[0:32, o_ones64c:o_ones64c + 64],
            a_ascr[0:32, 700 + j:701 + j], None, ALU.mult),
          ["ones64c", "wT"], ["wbc"])
        for g in range(4):
            bank = pf[4 + g]
            bname = PFB[4 + g]
            E("T", lambda e, g=g, bank=bank: e.matmul(
                bank[0:64, 0:512], AB[0:32, o_wbc:o_wbc + 64],
                c_sel2[:, g * 512:(g + 1) * 512], start=True, stop=True),
              ["wbc", "c_sel2"], [bname])
            if j == 0:
                E("V", lambda e, g=g, sl=sl, bank=bank: e.tensor_tensor(
                    AFp[0:64, o_sinA + g * 512:o_sinA + (g + 1) * 512],
                    AFp[0:64, o_sj + sl * 2048 + g * 512:
                        o_sj + sl * 2048 + (g + 1) * 512],
                    bank[0:64, 0:512], ALU.mult),
                  [f"sj{sl}", bname], ["sinA"])
            else:
                E("V", lambda e, g=g, sl=sl, bank=bank: e.tensor_tensor(
                    AFp[0:64, o_tmp + g * 512:o_tmp + (g + 1) * 512],
                    AFp[0:64, o_sj + sl * 2048 + g * 512:
                        o_sj + sl * 2048 + (g + 1) * 512],
                    bank[0:64, 0:512], ALU.mult),
                  [f"sj{sl}", bname], ["tmpc"])
                E("V", lambda e, g=g: e.tensor_tensor(
                    AFp[0:64, o_sinA + g * 512:o_sinA + (g + 1) * 512],
                    AFp[0:64, o_sinA + g * 512:o_sinA + (g + 1) * 512],
                    AFp[0:64, o_tmp + g * 512:o_tmp + (g + 1) * 512],
                    ALU.add), ["sinA", "tmpc"], ["sinA"])
    E("V", lambda e: ecopy(e, AB[0:64, o_sinb:o_sinb + 2048],
                           AFp[0:64, o_sinA:o_sinA + 2048]),
      ["sinA"], ["sinb"])
    E("V", lambda e: e.memset(AB[64:128, o_sinb:o_sinb + 2048], 0.0),
      [], ["sinbpad"])
    sbt["_o_sinb"] = o_sinb

def emit_p3(ctx, woutbf, u_ext, sp_y, sp_v, sp_z, sp_x1, sp_x0):
    P, E, pf, PFB = ctx["P"], ctx["E"], ctx["pf"], ctx["PFB"]
    arb, arf, AB, AFp, sbt = ctx["arb"], ctx["arf"], ctx["AB"], ctx["AF"], ctx["sbt"]
    c_ident_f = sbt["c_ident_f"]
    c_ones_row = sbt["c_ones_row"]
    c_ones_colb = sbt["c_ones_colb"]
    p_rmsw = sbt["p_rmsw"]
    o_sinb = sbt["_o_sinb"]

    P.barrier()
    arb.reset(2176)
    arf.reset()
    o_wout = arb.take(16 * 2048)
    o_ysl = arb.take(2048)
    o_vsl = arb.take(4096)
    o_zsl = arb.take(2 * 2048)
    o_sq = arb.take(2048)
    o_ynb = arb.take(2048)
    o_x1st = arb.take(2048)
    o_x0st = arb.take(2048)
    E("V", lambda e: e.memset(AB[64:128, o_vsl:o_vsl + 4096], 0.0),
      [], ["vslpad"])
    o_yg = arf.take(2048)
    o_ur = arf.take(2 * 2048)
    o_ln = arf.take(2048)
    o_x1f = arf.take(2048)
    o_rr = arf.take(128)
    o_cols = arf.take(16)

    for tc in range(NCHUNK):
        h = tc % 2
        E("Dw", lambda e, tc=tc: e.dma_start(
            out=AB[:, o_ysl:o_ysl + 2048],
            in_=sp_y[tc].rearrange("j p t -> p j t")),
          [f"spy{tc}"], ["ysl"])
        E("Dw", lambda e, tc=tc: e.dma_start(
            out=AB[0:64, o_vsl:o_vsl + 4096], in_=sp_v[tc]),
          [f"spv{tc}"], ["vsl"])
        E("Dw", lambda e, tc=tc, h=h: e.dma_start(
            out=AB[:, o_zsl + h * 2048:o_zsl + (h + 1) * 2048],
            in_=sp_z[:, :, tc * 128:(tc + 1) * 128].rearrange("a p t -> p a t")),
          [f"spz{i}" for i in range(16)], [f"zsl{h}"])
        E("Dw", lambda e, tc=tc, h=h: e.dma_start(
            out=AFp[:, o_ur + h * 2048:o_ur + (h + 1) * 2048],
            in_=u_ext[3 + tc * 128:3 + (tc + 1) * 128, :]), [], [f"ur{h}"])
        for g in range(4):
            bank = pf[g]
            bname = PFB[g]

            def yrmm(e, g=g, bank=bank):
                inst = None
                for q in range(4):
                    jj = g * 4 + q
                    for e_ in range(2):
                        hh = 2 * jj + e_
                        lhs = AB[:, o_sinb + hh * 64:o_sinb + (hh + 1) * 64]
                        rhs = AB[:, o_vsl + hh * 128:o_vsl + (hh + 1) * 128]
                        inst = e.matmul(bank[e_ * 64:(e_ + 1) * 64,
                                             q * 128:(q + 1) * 128],
                                        lhs, rhs, start=True, stop=True)
                return inst
            E("T", yrmm, ["sinb", "sinbpad", "vsl", "vslpad"], [bname])
            E("V", lambda e, g=g, bank=bank: e.tensor_tensor(
                AFp[:, o_yg + g * 512:o_yg + (g + 1) * 512], bank[:, 0:512],
                AB[:, o_ysl + g * 512:o_ysl + (g + 1) * 512],
                ALU.add), [bname, "ysl"], ["yg"])
            E("V", lambda e, g=g, h=h: e.tensor_tensor(
                AFp[:, o_yg + g * 512:o_yg + (g + 1) * 512],
                AFp[:, o_yg + g * 512:o_yg + (g + 1) * 512],
                AB[:, o_zsl + h * 2048 + g * 512:o_zsl + h * 2048 + (g + 1) * 512],
                ALU.mult), ["yg", f"zsl{h}"], ["yg"])
        E("S", lambda e: e.activation(AB[:, o_sq:o_sq + 2048],
                                      AFp[:, o_yg:o_yg + 2048], AF.Square),
          ["yg"], ["sq"])

        def sqmm(e):
            inst = None
            for jj in range(16):
                inst = e.matmul(pf[4][0:1, 0:128], c_ones_colb[:, :],
                                AB[:, o_sq + jj * 128:o_sq + (jj + 1) * 128],
                                start=(jj == 0), stop=(jj == 15))
            return inst
        E("T", sqmm, ["sq", "c_ones_colb"], [PFB[4]])
        E("V", lambda e: e.tensor_scalar(AFp[0:1, o_rr:o_rr + 128],
                                         pf[4][0:1, 0:128], 1.0 / 2048.0, EPS,
                                         ALU.mult, ALU.add), [PFB[4]], ["rr"])
        E("S", lambda e: e.activation(AFp[0:1, o_rr:o_rr + 128],
                                      AFp[0:1, o_rr:o_rr + 128], AF.Ln),
          ["rr"], ["rr"])
        E("S", lambda e: e.activation(AFp[0:1, o_rr:o_rr + 128],
                                      AFp[0:1, o_rr:o_rr + 128], AF.Exp,
                                      scale=-0.5), ["rr"], ["rr"])
        E("T", lambda e: e.matmul(pf[5][:, 0:128], c_ones_row[:, :],
                                  AFp[0:1, o_rr:o_rr + 128],
                                  start=True, stop=True),
          ["rr", "c_ones_row"], [PFB[5]])
        for jj in range(16):
            E("V", lambda e, jj=jj: e.scalar_tensor_tensor(
                AB[:, o_ynb + jj * 128:o_ynb + (jj + 1) * 128],
                AFp[:, o_yg + jj * 128:o_yg + (jj + 1) * 128],
                p_rmsw[:, jj:jj + 1], pf[5][:, 0:128], ALU.mult, ALU.mult),
              ["yg", "p_rmsw", PFB[5]], ["ynb"])
        for nb in range(4):
            bank = pf[nb]
            bname = PFB[nb]

            def womm(e, nb=nb, bank=bank):
                inst = None
                for ec in range(16):
                    inst = e.matmul(bank[:, 0:512],
                                    AB[:, o_ynb + ec * 128:o_ynb + (ec + 1) * 128],
                                    AB[:, o_wout + ec * 2048 + nb * 512:
                                       o_wout + ec * 2048 + (nb + 1) * 512],
                                    start=(ec == 0), stop=(ec == 15))
                return inst
            E("T", womm, ["ynb", "woutres"], [bname])
            E("V", lambda e, nb=nb, h=h, bank=bank: e.tensor_tensor(
                AFp[:, o_ln + nb * 512:o_ln + (nb + 1) * 512], bank[:, 0:512],
                AFp[:, o_ur + h * 2048 + nb * 512:o_ur + h * 2048 + (nb + 1) * 512],
                ALU.add), [bname, f"ur{h}"], ["lnb"])
        E("V", lambda e: e.reduce_sum(AFp[:, o_cols:o_cols + 1],
                                      AFp[:, o_ln:o_ln + 2048], AX.X),
          ["lnb"], ["mcol"])
        E("V", lambda e: e.tensor_scalar_mul(AFp[:, o_cols:o_cols + 1],
                                             AFp[:, o_cols:o_cols + 1],
                                             1.0 / 2048.0), ["mcol"], ["mcol"])
        E("V", lambda e: e.tensor_scalar(
            AFp[:, o_ln:o_ln + 2048], AFp[:, o_ln:o_ln + 2048],
            AFp[:, o_cols:o_cols + 1], None, ALU.subtract),
          ["lnb", "mcol"], ["lnb"])
        E("S", lambda e: e.activation(AB[:, o_sq:o_sq + 2048],
                                      AFp[:, o_ln:o_ln + 2048], AF.Square),
          ["lnb"], ["sq"])
        E("V", lambda e: e.reduce_sum(AFp[:, o_cols + 1:o_cols + 2],
                                      AB[:, o_sq:o_sq + 2048], AX.X),
          ["sq"], ["vcol"])
        E("V", lambda e: e.tensor_scalar(AFp[:, o_cols + 1:o_cols + 2],
                                         AFp[:, o_cols + 1:o_cols + 2],
                                         1.0 / 2048.0, EPS, ALU.mult, ALU.add),
          ["vcol"], ["vcol"])
        E("S", lambda e: e.activation(AFp[:, o_cols + 1:o_cols + 2],
                                      AFp[:, o_cols + 1:o_cols + 2], AF.Ln),
          ["vcol"], ["vcol"])
        E("S", lambda e: e.activation(AFp[:, o_cols + 1:o_cols + 2],
                                      AFp[:, o_cols + 1:o_cols + 2], AF.Exp,
                                      scale=-0.5), ["vcol"], ["vcol"])
        E("V", lambda e: e.tensor_scalar(
            AFp[:, o_x1f:o_x1f + 2048], AFp[:, o_ln:o_ln + 2048],
            AFp[:, o_cols + 1:o_cols + 2], None, ALU.mult),
          ["lnb", "vcol"], ["x1f"])
        E("V", lambda e: ecopy(e,
            AB[:, o_x1st:o_x1st + 2048],
            AFp[:, o_x1f:o_x1f + 2048]), ["x1f"], ["x1st"])
        E("Ds", lambda e, tc=tc: e.dma_start(
            out=sp_x1[tc], in_=AB[:, o_x1st:o_x1st + 2048]),
          ["x1st"], [f"spx1_{tc}"])
        for g in range(4):
            bank = pf[6 + (g % 2)]
            bname = PFB[6 + (g % 2)]

            def trx0(e, g=g, bank=bank):
                inst = None
                for k in range(4):
                    dk = g * 4 + k
                    inst = e.transpose(bank[:, k * 128:(k + 1) * 128],
                                       AFp[:, o_x1f + dk * 128:
                                           o_x1f + (dk + 1) * 128],
                                       c_ident_f[:, :])
                return inst
            E("T", trx0, ["x1f", "c_ident_f"], [bname])
            E("S" if g % 2 == 0 else "V",
              lambda e, g=g, bank=bank: ecopy(e, 
                  AB[:, o_x0st + g * 512:o_x0st + (g + 1) * 512],
                  bank[:, 0:512]), [bname], ["x0st"])
        E("Ds", lambda e, tc=tc: e.dma_start(
            out=sp_x0[:, :, tc * 128:(tc + 1) * 128].rearrange("d p t -> p d t"),
            in_=AB[:, o_x0st:o_x0st + 2048]),
          ["x0st"], [f"spx0_{tc}"])


def emit_p4(ctx, wupB, wgB, wdnR, sp_x2, sp_x0, sp_x1, out):
    P, E, pf, PFB = ctx["P"], ctx["E"], ctx["pf"], ctx["PFB"]
    arb, arf, AB, AFp, sbt = ctx["arb"], ctx["arf"], ctx["AB"], ctx["AF"], ctx["sbt"]
    s_hsu16 = sbt["s_hsu16"]
    s_hsu64 = sbt["s_hsu64"]
    p_wsuup = sbt["p_wsuup"]
    p_wsugt = sbt["p_wsugt"]
    c_ones_row = sbt["c_ones_row"]
    a_ascr = sbt["a_ascr"]

    P.barrier()
    arb.reset()
    arf.reset()
    o_x0sl = arb.take(2 * 512)
    o_x0h = arb.take(16 * 512)
    o_x0hm = arb.take(16 * 512)
    o_x2st = arb.take(2 * 512)
    o_x2l = arb.take(4 * 256)
    o_wup = arb.take(2 * 2048)
    o_wg = arb.take(2 * 2048)
    o_wdn = arb.take(4 * 2048)
    o_scr = arb.take(2048)
    o_x1c = arb.take(2 * 2048)
    o_ln = arf.take(2 * 2048)
    o_out = arf.take(2048)
    o_wsdb = arf.take(2048)
    o_cols = arf.take(16)


    wsd_row = ctx["wsd_row_ap"]
    E("Dw", lambda e: e.dma_start(out=AFp[0:1, o_wsdb:o_wsdb + 2048], in_=wsd_row),
      [], ["wsdrow"])
    for i in range(4):
        E("T", lambda e, i=i: e.matmul(
            pf[i][:, :], c_ones_row[:, :],
            AFp[0:1, o_wsdb + i * 512:o_wsdb + (i + 1) * 512],
            start=True, stop=True),
          ["wsdrow", "c_ones_row"], [PFB[i]])
    for i in range(4):
        E("V", lambda e, i=i: e.tensor_scalar_mul(
            AFp[:, o_wsdb + i * 512:o_wsdb + (i + 1) * 512], pf[i][:, :],
            64.0 * CORR64), [PFB[i]], ["p_wsdb", "wsdrow"])

    bounce_done = [f"fw0_{i}" for i in range(128)] + \
                  [f"fw1_{i}" for i in range(128)]
    dv = wdnR.rearrange("(u ah) (l d) -> u (ah l) d", u=64, ah=2, l=64, d=2048)
    h16m = hadamard(16)

    for s in range(NSUPER):
        for dc in range(16):
            hh = dc % 2
            E("Dw", lambda e, s=s, dc=dc, hh=hh: e.dma_start(
                out=AB[:, o_x0sl + hh * 512:o_x0sl + (hh + 1) * 512],
                in_=sp_x0[dc, :, s * 512:(s + 1) * 512]),
              [f"spx0_{tcx}" for tcx in range(s * 4, s * 4 + 4)], [f"x0sl{hh}"])
            bank = pf[4 + (dc % 2)]
            bname = PFB[4 + (dc % 2)]
            E("T", lambda e, dc=dc, hh=hh, bank=bank: e.matmul(
                bank[:, 0:512], s_hsu16[:, dc * 128:(dc + 1) * 128],
                AB[:, o_x0sl + hh * 512:o_x0sl + (hh + 1) * 512],
                start=True, stop=True),
              [f"x0sl{hh}", "s_hsu16"], [bname])
            E("V" if dc % 2 == 0 else "S",
              lambda e, dc=dc, bank=bank: ecopy(e,
                  AB[:, o_x0h + dc * 512:o_x0h + (dc + 1) * 512], bank[:, 0:512]),
              [bname], ["x0h"])
        for dcp in range(16):
            bank = pf[dcp % 4]
            bname = PFB[dcp % 4]

            def mixmm(e, dcp=dcp, bank=bank):
                inst = None
                for dc in range(16):
                    lhs = sbt["c_m8p"] if h16m[dc, dcp] > 0 else sbt["c_m8n"]
                    inst = e.matmul(bank[:, 0:512], lhs[:, :],
                                    AB[:, o_x0h + dc * 512:o_x0h + (dc + 1) * 512],
                                    start=(dc == 0), stop=(dc == 15))
                return inst
            E("T", mixmm, ["x0h", "c_m8p", "c_m8n"], [bname])
            E("V" if dcp % 2 == 0 else "S",
              lambda e, dcp=dcp, bank=bank: ecopy(e,
                  AB[:, o_x0hm + dcp * 512:o_x0hm + (dcp + 1) * 512],
                  bank[:, 0:512]), [bname], ["x0hm"])

        def emit_hsu64(f, s=s):
            so = 0
            hb = 6 + (f % 2)
            E("T", lambda e, f=f, hb=hb, so=so: e.matmul(
                pf[hb][:, 0:512], s_hsu64[:, f * 128:(f + 1) * 128],
                AB[:, o_scr + so + 1024:o_scr + so + 1536], start=True, stop=True),
              [f"x2p{f%2}", "s_hsu64"], [PFB[hb]])
            E("V" if f % 2 == 0 else "S",
              lambda e, f=f, hb=hb: ecopy(e,
                  AB[:, o_x2st + (f % 2) * 512:o_x2st + (f % 2 + 1) * 512],
                  pf[hb][:, 0:512]), [PFB[hb]], [f"x2st{f%2}"])
            E("Ds", lambda e, f=f, s=s: e.dma_start(
                out=sp_x2[s, f],
                in_=AB[:, o_x2st + (f % 2) * 512:o_x2st + (f % 2 + 1) * 512]),
              [f"x2st{f%2}"], [f"spx2_{s}_{f}"])

        for f in range(64):
            hh = f % 2
            E("Dw", lambda e, f=f, hh=hh: e.dma_start(
                out=AB[:, o_wup + hh * 2048:o_wup + (hh + 1) * 2048],
                in_=wupB[:, f * 2048:(f + 1) * 2048]), bounce_done,
              [f"wupp{hh}"])
            E("Dp", lambda e, f=f, hh=hh: e.dma_start(
                out=AB[:, o_wg + hh * 2048:o_wg + (hh + 1) * 2048],
                in_=wgB[:, f * 2048:(f + 1) * 2048]), bounce_done,
              [f"wgp{hh}"])
            fb = (f % 2) * 2

            def upmm(e, hh=hh, fb=fb):
                inst = None
                for dc in range(16):
                    inst = e.matmul(pf[0 + fb][:, 0:512],
                                    AB[:, o_wup + hh * 2048 + dc * 128:
                                       o_wup + hh * 2048 + (dc + 1) * 128],
                                    AB[:, o_x0hm + dc * 512:o_x0hm + (dc + 1) * 512],
                                    start=(dc == 0), stop=(dc == 15))
                return inst
            E("T", upmm, [f"wupp{hh}", "x0hm"], [PFB[0 + fb]])

            def gtmm(e, hh=hh, fb=fb):
                inst = None
                for dc in range(16):
                    inst = e.matmul(pf[1 + fb][:, 0:512],
                                    AB[:, o_wg + hh * 2048 + dc * 128:
                                       o_wg + hh * 2048 + (dc + 1) * 128],
                                    AB[:, o_x0hm + dc * 512:o_x0hm + (dc + 1) * 512],
                                    start=(dc == 0), stop=(dc == 15))
                return inst
            E("T", gtmm, [f"wgp{hh}", "x0hm"], [PFB[1 + fb]])
            if f > 0:
                emit_hsu64(f - 1)
            so = 0
            E("S", lambda e, f=f, fb=fb, so=so: e.activation(
                AB[:, o_scr + so:o_scr + so + 512], pf[1 + fb][:, 0:512], AF.Silu,
                scale=p_wsugt[:, f:f + 1]),
              [PFB[1 + fb], "p_wsugt"], [f"gsc{f%2}"])
            E("V", lambda e, f=f, fb=fb, so=so: e.tensor_scalar(
                AB[:, o_scr + so + 512:o_scr + so + 1024], pf[0 + fb][:, 0:512],
                p_wsuup[:, f:f + 1], None, ALU.mult),
              [PFB[0 + fb], "p_wsuup"], [f"usc{f%2}"])
            E("V", lambda e, so=so: e.tensor_tensor(
                AB[:, o_scr + so + 1024:o_scr + so + 1536],
                AB[:, o_scr + so:o_scr + so + 512],
                AB[:, o_scr + so + 512:o_scr + so + 1024], ALU.mult),
              [f"gsc{f%2}", f"usc{f%2}"], [f"x2p{f%2}"])
        emit_hsu64(63)
        for tp in range(2):
            def ldw(f, s=s, tp=tp):
                hh = f % 4
                E("Dp", lambda e, f=f, hh=hh: e.dma_start(
                    out=AB[:, o_wdn + hh * 2048:o_wdn + (hh + 1) * 2048],
                    in_=dv[f]), [f"fw2_{pp}" for pp in range(64)], [f"wdnp{hh}"])
                E("Dw", lambda e, f=f, hh=hh, s=s, tp=tp: e.dma_start(
                    out=AB[:, o_x2l + hh * 256:o_x2l + (hh + 1) * 256],
                    in_=sp_x2[s, f, :, tp * 256:(tp + 1) * 256]),
                  [f"spx2_{s}_{f}"], [f"x2l{hh}"])
            for f in range(3):
                ldw(f)
            for f in range(64):
                hh = f % 4
                if f + 3 < 64:
                    ldw(f + 3)

                def dnmm(e, f=f, hh=hh):
                    inst = None
                    for ti in range(2):
                        for nb in range(4):
                            bank = pf[ti * 4 + nb]
                            lhs = AB[:, o_x2l + hh * 256 + ti * 128:
                                     o_x2l + hh * 256 + (ti + 1) * 128]
                            rhs = AB[:, o_wdn + hh * 2048 + nb * 512:
                                     o_wdn + hh * 2048 + (nb + 1) * 512]
                            inst = e.matmul(bank[:, 0:512], lhs, rhs,
                                            start=(f == 0), stop=(f == 63))
                    return inst
                E("T", dnmm, [f"wdnp{hh}", f"x2l{hh}"], [PFB[i] for i in range(8)])
            for ti in range(2):
                tc = s * 4 + tp * 2 + ti
                E("Dw", lambda e, tc=tc, ti=ti: e.dma_start(
                    out=AB[:, o_x1c + ti * 2048:o_x1c + (ti + 1) * 2048],
                    in_=sp_x1[tc]),
                  [f"spx1_{tc}"], [f"x1c{ti}"])
            for ti in range(2):
                for nb in range(4):
                    bank = pf[ti * 4 + nb]
                    bname = PFB[ti * 4 + nb]
                    E("V", lambda e, ti=ti, nb=nb, bank=bank: e.tensor_tensor(
                        AFp[:, o_ln + ti * 2048 + nb * 512:
                            o_ln + ti * 2048 + (nb + 1) * 512],
                        bank[:, 0:512],
                        AFp[:, o_wsdb + nb * 512:o_wsdb + (nb + 1) * 512],
                        ALU.mult), [bname, "p_wsdb"], [f"lnb4_{ti}"])
            for ti in range(2):
                tc = s * 4 + tp * 2 + ti
                ol = o_ln + ti * 2048
                oc = o_cols + ti * 8
                for nb in range(4):
                    E("V", lambda e, ti=ti, nb=nb, ol=ol: e.tensor_tensor(
                        AFp[:, ol + nb * 512:ol + (nb + 1) * 512],
                        AFp[:, ol + nb * 512:ol + (nb + 1) * 512],
                        AB[:, o_x1c + ti * 2048 + nb * 512:
                           o_x1c + ti * 2048 + (nb + 1) * 512],
                        ALU.add), [f"lnb4_{ti}", f"x1c{ti}"], [f"lnb4_{ti}"])
                E("V", lambda e, ol=ol, oc=oc: e.reduce_sum(
                    AFp[:, oc:oc + 1], AFp[:, ol:ol + 2048], AX.X),
                  [f"lnb4_{ti}"], [f"mc2_{ti}"])
                E("V", lambda e, oc=oc: e.tensor_scalar_mul(
                    AFp[:, oc:oc + 1], AFp[:, oc:oc + 1],
                    1.0 / 2048.0), [f"mc2_{ti}"], [f"mc2_{ti}"])
                E("V", lambda e, ol=ol, oc=oc: e.tensor_scalar(
                    AFp[:, ol:ol + 2048], AFp[:, ol:ol + 2048],
                    AFp[:, oc:oc + 1], None, ALU.subtract),
                  [f"lnb4_{ti}", f"mc2_{ti}"], [f"lnb4_{ti}"])
                E("S", lambda e, ol=ol: e.activation(
                    AB[:, o_scr:o_scr + 2048], AFp[:, ol:ol + 2048],
                    AF.Square), [f"lnb4_{ti}"], ["sq2"])
                E("V", lambda e, oc=oc: e.reduce_sum(
                    AFp[:, oc + 1:oc + 2], AB[:, o_scr:o_scr + 2048], AX.X),
                  ["sq2"], [f"vc2_{ti}"])
                E("V", lambda e, oc=oc: e.tensor_scalar(
                    AFp[:, oc + 1:oc + 2], AFp[:, oc + 1:oc + 2],
                    1.0 / 2048.0, EPS, ALU.mult, ALU.add),
                  [f"vc2_{ti}"], [f"vc2_{ti}"])
                E("S", lambda e, oc=oc: e.activation(
                    AFp[:, oc + 1:oc + 2], AFp[:, oc + 1:oc + 2],
                    AF.Ln), [f"vc2_{ti}"], [f"vc2_{ti}"])
                E("S", lambda e, oc=oc: e.activation(
                    AFp[:, oc + 1:oc + 2], AFp[:, oc + 1:oc + 2],
                    AF.Exp, scale=-0.5), [f"vc2_{ti}"], [f"vc2_{ti}"])
                E("V", lambda e, ol=ol, oc=oc: e.tensor_scalar(
                    AFp[:, o_out:o_out + 2048],
                    AFp[:, ol:ol + 2048],
                    AFp[:, oc + 1:oc + 2], None, ALU.mult),
                  [f"lnb4_{ti}", f"vc2_{ti}"], ["outst"])
                E("Dp", lambda e, tc=tc: e.dma_start(
                    out=out[tc * 128:(tc + 1) * 128, :],
                    in_=AFp[:, o_out:o_out + 2048]),
                  ["outst"], [f"outc{tc}"])


_BUILD_CACHE = {}


def kernel(**inputs):
    shared = shared_weight_inputs(inputs)
    consts = host_constants()
    in_maps = []
    for cid in range(NCORES):
        m = {}
        m.update(shared)
        m.update(consts)
        m.update(per_core_inputs(inputs, cid))
        in_maps.append(m)
    nc = build_program()
    res = run_bass_kernel_spmd(nc, in_maps, core_ids=list(range(NCORES)))
    full = np.zeros((2, 8192, 2048), np.float32)
    for cid in range(NCORES):
        full[cid // 4, (cid % 4) * TOK:(cid % 4 + 1) * TOK] = \
            res.results[cid]["out"]
    return full


# version 23
# speedup vs baseline: 1.1664x; 1.1664x over previous
import numpy as np
from contextlib import ExitStack

import concourse.bass as bass
import concourse.mybir as mybir
from concourse.bass_utils import run_bass_kernel_spmd

F32 = mybir.dt.float32
BF16 = mybir.dt.bfloat16
AF = mybir.ActivationFunctionType
ALU = mybir.AluOpType
AX = mybir.AxisListType

D_MODEL = 2048
CONV_DIM = 6144
PROJ_DIM = 8224
INTER = 8192
NCORES = 8
TOK = 2048
NCHUNK = 16
NXT = 48
NPAIR = 16
SUPER = 512
NSUPER = TOK // SUPER
JPAD = 2052

EPS = 1e-5
CORR16 = float((1.0 / (64.0 * np.sqrt(2048.0))) / (2.0 ** -12))
CORR64 = float((1.0 / (64.0 * np.sqrt(8192.0))) / (2.0 ** -12))

PHASES = 99
SPILLS_EXTERNAL = False
DEBUG_SSD = False
UB_LIM = 17
SSD_LIM = 16
SSD_SECT = 5
PAIR_SECT = 9



def ecopy(e, out, in_):
    if hasattr(e, "tensor_copy"):
        return e.tensor_copy(out, in_)
    return e.copy(out, in_)

class Buf:
    __slots__ = ("name", "w", "readers")

    def __init__(self, name):
        self.name = name
        self.w = None
        self.readers = {}


NSLOT = 8


class Prog:
    DMA_STREAMS = {"Dw": 16, "Ds": 16, "Dp": 16, "CC": 1}
    ENGINE_OF = {"T": "tensor", "V": "vector", "S": "scalar", "P": "gpsimd",
                 "CC": "gpsimd", "Dw": "sync", "Ds": "scalar", "Dp": "gpsimd"}

    def __init__(self):
        self.counts = {s: 0 for s in ("T", "V", "S", "P", "CC", "Dw", "Ds", "Dp")}
        self.steps = []
        self.last_wait = {}

    def _emit_wait(self, eng, sems, consumer, s, c):
        if s in self.DMA_STREAMS:
            step = self.DMA_STREAMS[s]
            for r in range(NSLOT):
                if c < r + 1:
                    continue
                i_r = c - ((c - 1 - r) % NSLOT)
                val = step * ((i_r - 1) // NSLOT + 1)
                k = (consumer, s, r)
                if self.last_wait.get(k, 0) < val:
                    eng.wait_ge(sems[(s, r)], val)
                    self.last_wait[k] = val
        else:
            k = (consumer, s)
            if self.last_wait.get(k, 0) < c:
                eng.wait_ge(sems[s], c)
                self.last_wait[k] = c

    def emit(self, stream, fn, reads=(), writes=()):
        need = {}
        for b in reads:
            if b.w is not None:
                s, c = b.w
                need[s] = max(need.get(s, 0), c)
        for b in writes:
            if b.w is not None:
                s, c = b.w
                need[s] = max(need.get(s, 0), c)
            for s, c in b.readers.items():
                need[s] = max(need.get(s, 0), c)
        waits = {}
        for s, c in need.items():
            if s == stream == "T":
                continue
            waits[s] = c
        self.counts[stream] += 1
        tok = (stream, self.counts[stream])
        for b in writes:
            b.w = tok
            b.readers = {}
        for b in reads:
            b.readers[stream] = self.counts[stream]
        self.steps.append((stream, waits, fn))
        return tok

    def barrier(self):
        snap = dict(self.counts)
        for stream in list(self.counts):
            self.steps.append((stream, dict(snap), ("BARRIER",)))

    def final_waits(self):
        snap = dict(self.counts)
        self.steps.append(("Dp", dict(snap), ("BARRIER",)))

    def realize(self, block, sems):
        per_engine = {"tensor": [], "vector": [], "scalar": [], "gpsimd": [],
                      "sync": []}
        idx = {s: 0 for s in self.counts}
        for stream, waits, fn in self.steps:
            if not isinstance(fn, tuple):
                idx[stream] += 1
            per_engine[self.ENGINE_OF[stream]].append(
                (stream, idx[stream], waits, fn))

        def make_runner(items):
            def runner(eng_obj):
                for stream, myi, waits, fn in items:
                    for s, c in waits.items():
                        if (s == stream and s not in self.DMA_STREAMS
                                and isinstance(fn, tuple)):
                            continue
                        self._emit_wait(eng_obj, sems, stream, s, c)
                    if isinstance(fn, tuple):
                        continue
                    if stream in self.DMA_STREAMS:
                        step = self.DMA_STREAMS[stream]
                        slot = (myi - 1) % NSLOT
                        prev = step * ((myi - 1) // NSLOT)
                        if prev > 0:
                            k = (stream, stream, slot)
                            if self.last_wait.get(k, 0) < prev:
                                eng_obj.wait_ge(sems[(stream, slot)], prev)
                                self.last_wait[k] = prev
                        inst = fn(eng_obj)
                        inst.then_inc(sems[(stream, slot)], step)
                    else:
                        inst = fn(eng_obj)
                        inst.then_inc(sems[stream], 1)
            return runner

        block.tensor(make_runner(per_engine["tensor"]))
        block.vector(make_runner(per_engine["vector"]))
        block.scalar(make_runner(per_engine["scalar"]))
        block.gpsimd(make_runner(per_engine["gpsimd"]))
        block.sync(make_runner(per_engine["sync"]))


class Arena:
    def __init__(self, width):
        self.width = width
        self.off = 0

    def reset(self, at=0):
        self.off = at

    def take(self, n):
        assert self.off + n <= self.width, f"arena overflow {self.off}+{n}"
        o = self.off
        self.off += n
        return o


def hadamard(n):
    h = np.array([[1.0]])
    while h.shape[0] < n:
        h = np.block([[h, h], [h, -h]])
    return h


def host_constants():
    c = {}
    c["cident"] = np.eye(128, dtype=np.float32)
    c["ctril"] = np.triu(np.ones((128, 128), np.float32))
    c["cgmask"] = np.triu(np.ones((128, 128), np.float32))
    c["cgmask2"] = np.concatenate([c["cgmask"], c["cgmask"]], axis=1)
    c["ch128"] = hadamard(128).astype(np.float32)
    hb16 = np.zeros((128, 128), np.float32)
    h16 = hadamard(16) * (2.0 ** -12)
    for b in range(8):
        hb16[b * 16:(b + 1) * 16, b * 16:(b + 1) * 16] = h16
    c["chblk16"] = hb16
    hb64 = np.zeros((128, 128), np.float32)
    h64 = hadamard(64) * (2.0 ** -12)
    for b in range(2):
        hb64[b * 64:(b + 1) * 64, b * 64:(b + 1) * 64] = h64
    c["chblk64"] = hb64
    sel = np.zeros((32, NPAIR * 128), np.float32)
    for j in range(NPAIR):
        sel[2 * j, j * 128:j * 128 + 64] = 1.0
        sel[2 * j + 1, j * 128 + 64:j * 128 + 128] = 1.0
    c["cselpack"] = sel
    sel2 = np.zeros((32, 32 * 64), np.float32)
    for hh in range(32):
        sel2[hh, hh * 64:(hh + 1) * 64] = 1.0
    c["csel2pack"] = sel2
    h8 = hadamard(8)
    m8 = np.zeros((128, 128), np.float32)
    for i in range(128):
        for j in range(128):
            if (i & 15) == (j & 15):
                m8[i, j] = h8[i >> 4, j >> 4]
    c["cm8"] = m8
    c["cones_row"] = np.ones((1, 128), np.float32)
    return c


def per_core_inputs(inputs, cid):
    u = np.asarray(inputs["L0_attn_u"], np.float32)
    b = cid // 4
    t0 = (cid % 4) * TOK
    u_ext = np.zeros((2176, D_MODEL), np.float32)
    lo = max(t0 - 3, 0)
    u_ext[3 - (t0 - lo):3 + TOK] = u[b, lo:t0 + TOK]
    mypos = cid % 4
    msel_t = np.zeros((8, 8), np.float32)
    agmask = np.zeros((8, 1), np.float32)
    for j in range(8):
        if j // 4 == b and (j % 4) < mypos:
            agmask[j, 0] = 1.0
            for m in range(8):
                if m // 4 == b and (j % 4) < (m % 4) < mypos:
                    msel_t[m, j] = 1.0
    return {"u_ext": u_ext, "msel_t": msel_t, "agmask": agmask}


def shared_weight_inputs(inputs):
    W_in = np.asarray(inputs["L0_attn_W_in"], np.float32)
    W_out = np.asarray(inputs["L0_attn_W_out"], np.float32)
    W_up = np.asarray(inputs["L0_ffn_W_up"], np.float32)
    W_gate = np.asarray(inputs["L0_ffn_W_gate"], np.float32)
    W_down = np.asarray(inputs["L0_ffn_W_down"], np.float32)
    d = {}
    W_in_pad = np.zeros((8320, 2048), np.float32)
    W_in_pad[:8224] = W_in
    d["w_in_p"] = np.ascontiguousarray(
        W_in_pad.reshape(65, 128, 16, 128).transpose(3, 0, 2, 1)
        .reshape(128, 65 * 2048))
    d["w_out_t"] = np.ascontiguousarray(W_out.T)
    d["w_up_p"] = np.ascontiguousarray(
        W_up.reshape(64, 128, 16, 128).transpose(3, 0, 2, 1)
        .reshape(128, 64 * 2048))
    d["w_gate_p"] = np.ascontiguousarray(
        W_gate.reshape(64, 128, 16, 128).transpose(3, 0, 2, 1)
        .reshape(128, 64 * 2048))
    d["w_down_f"] = np.ascontiguousarray(
        W_down.reshape(D_MODEL, 128, 64).transpose(1, 2, 0).reshape(128, 64 * D_MODEL))
    d["z_bias_fm"] = np.ascontiguousarray(
        np.asarray(inputs["L0_attn_z_bias"], np.float32).reshape(16, 128).T)
    d["conv_w_fm"] = np.ascontiguousarray(
        np.asarray(inputs["L0_attn_conv_w"], np.float32).reshape(48, 128, 4)
        .transpose(1, 0, 2).reshape(128, 192))
    d["conv_b_fm"] = np.ascontiguousarray(
        np.asarray(inputs["L0_attn_conv_b"], np.float32).reshape(48, 128).T)
    Dv = np.asarray(inputs["L0_attn_D"], np.float32)
    dpair = np.zeros((128, NPAIR), np.float32)
    for j in range(NPAIR):
        dpair[0:64, j] = Dv[2 * j]
        dpair[64:128, j] = Dv[2 * j + 1]
    d["d_pair"] = dpair
    d["rms_w_fm"] = np.ascontiguousarray(
        np.asarray(inputs["L0_attn_rms_w"], np.float32).reshape(16, 128).T)
    d["su_ug_fm"] = np.ascontiguousarray(
        np.asarray(inputs["L0_ffn_SU_ug"], np.float32).reshape(16, 128).T)
    d["su_dp_fm"] = np.ascontiguousarray(
        np.asarray(inputs["L0_ffn_SU_dp"], np.float32).reshape(64, 128).T)
    wsug = np.asarray(inputs["L0_ffn_Wscale_ug"], np.float32)
    d["wsu_up_fm"] = np.ascontiguousarray(wsug[:INTER].reshape(64, 128).T)
    d["wsu_gate_fm"] = np.ascontiguousarray(wsug[INTER:].reshape(64, 128).T)
    d["wsd_row"] = np.ascontiguousarray(
        np.asarray(inputs["L0_ffn_Wscale_dp"], np.float32).reshape(1, 2048))
    return d


def build_program():
    nc = bass.Bass("TRN2", target_bir_lowering=False)
    P = Prog()
    bufs = {}

    def buf(name):
        if name not in bufs:
            bufs[name] = Buf(name)
        return bufs[name]

    def E(stream, fn, reads, writes):
        P.emit(stream, fn, [buf(r) for r in reads], [buf(w) for w in writes])

    def din(name, shape, dt=F32):
        return nc.dram_tensor(name, shape, dt, kind="ExternalInput").ap()

    u_ext = din("u_ext", [2176, D_MODEL])
    w_in_p = din("w_in_p", [128, 65 * 2048])
    w_out_t = din("w_out_t", [D_MODEL, D_MODEL])
    w_up_p = din("w_up_p", [128, 64 * 2048])
    w_gate_p = din("w_gate_p", [128, 64 * 2048])
    w_down_f = din("w_down_f", [128, 64 * D_MODEL])
    z_bias_fm = din("z_bias_fm", [128, 16])
    conv_w_fm = din("conv_w_fm", [128, 192])
    conv_b_fm = din("conv_b_fm", [128, 48])
    d_pair = din("d_pair", [128, NPAIR])
    rms_w_fm = din("rms_w_fm", [128, 16])
    su_ug_fm = din("su_ug_fm", [128, 16])
    su_dp_fm = din("su_dp_fm", [128, 64])
    wsu_up_fm = din("wsu_up_fm", [128, 64])
    wsu_gate_fm = din("wsu_gate_fm", [128, 64])
    wsd_row = din("wsd_row", [1, 2048])
    msel_t = din("msel_t", [8, 8])
    agmask_i = din("agmask", [8, 1])
    cident = din("cident", [128, 128])
    ctril = din("ctril", [128, 128])
    cgmask = din("cgmask", [128, 128])
    cgmask2 = din("cgmask2", [128, 256])
    ch128 = din("ch128", [128, 128])
    chblk16 = din("chblk16", [128, 128])
    chblk64 = din("chblk64", [128, 128])
    cselpack = din("cselpack", [32, NPAIR * 128])
    csel2pack = din("csel2pack", [32, 32 * 64])
    cones_row = din("cones_row", [1, 128])
    cm8 = din("cm8", [128, 128])

    out = nc.dram_tensor("out", [TOK, D_MODEL], F32, kind="ExternalOutput").ap()

    spill_kind = "ExternalOutput" if SPILLS_EXTERNAL else "Internal"

    def dsp(name, shape, dt):
        return nc.dram_tensor(name, shape, dt, kind=spill_kind).ap()

    sp_conv = dsp("sp_conv", [NXT, 128, TOK], BF16)
    sp_xbtm = dsp("sp_xbtm", [32, NCHUNK, 128, 128], BF16)
    sp_z = dsp("sp_z", [16, 128, TOK], BF16)
    sp_y = dsp("sp_y", [NCHUNK, 128, NPAIR * 128], BF16)
    sp_v = dsp("sp_v", [NCHUNK, 64, 4096], BF16)
    sp_x1 = dsp("sp_x1", [NCHUNK, 128, D_MODEL], BF16)
    sp_x0 = dsp("sp_x0", [16, 128, TOK], BF16)
    sp_x2 = nc.dram_tensor("sp_x2", [NSUPER, 64, 128, SUPER], BF16).ap()
    wupB = nc.dram_tensor("wupB", [128, 64 * 2048], BF16).ap()
    wgB = nc.dram_tensor("wgB", [128, 64 * 2048], BF16).ap()
    wdnR = nc.dram_tensor("wdnR", [128, 64 * D_MODEL], BF16).ap()
    agin = nc.dram_tensor("agin", [64, 2056], F32).ap()
    dbg_gm = nc.dram_tensor("dbg_gm", [NPAIR, 128, 256], BF16).ap()
    dbg_xtme = nc.dram_tensor("dbg_xtme", [128, 2048], BF16).ap()
    dbg_eb = nc.dram_tensor("dbg_eb", [128, 2048], BF16).ap()
    dbg_ytmp = nc.dram_tensor("dbg_ytmp", [NPAIR, 128, 128], F32).ap()
    ctxdbg = dict(dbg_gm=dbg_gm, dbg_xtme=dbg_xtme, dbg_eb=dbg_eb, dbg_ytmp=dbg_ytmp)
    agout = nc.dram_tensor("agout", [64 * NCORES, 2056], F32).ap()

    stack = ExitStack()
    sems = {}
    for s in ("T", "V", "S", "P"):
        sems[s] = stack.enter_context(nc.semaphore("sem_" + s))
    for s in ("CC", "Dw", "Ds", "Dp"):
        for r in range(NSLOT):
            sems[(s, r)] = stack.enter_context(nc.semaphore(f"sem_{s}{r}"))

    pf = [stack.enter_context(nc.psum_tensor(f"pf{i}", [128, 512], F32))
          for i in range(4)]
    pf45 = stack.enter_context(nc.psum_tensor("pf45", [128, 1024], F32))
    pf += [pf45[:, 0:512], pf45[:, 512:1024]]
    pf += [stack.enter_context(nc.psum_tensor(f"pf{i}", [128, 512], F32))
           for i in range(6, 8)]
    PFB = [f"PSB{i}" for i in range(8)]
    PFB45 = "PSB45"

    sbt = {}

    def alloc(name, shape, dt):
        sbt[name] = stack.enter_context(nc.sbuf_tensor(name, shape, dt))
        buf(name)
        return sbt[name]

    c_ident_f = alloc("c_ident_f", [128, 128], F32)
    c_tril = alloc("c_tril", [128, 128], F32)
    c_gmask2 = alloc("c_gmask2", [128, 256], F32)
    c_h128f = alloc("c_h128f", [128, 128], F32)
    c_m8p = alloc("c_m8p", [128, 128], BF16)
    c_m8n = alloc("c_m8n", [128, 128], BF16)
    c_hblk16 = alloc("c_hblk16", [128, 128], BF16)
    c_hblk64 = alloc("c_hblk64", [128, 128], BF16)
    c_sel = alloc("c_sel", [32, NPAIR * 128], BF16)
    c_sel2 = alloc("c_sel2", [32, 32 * 64], BF16)
    c_ones_row = alloc("c_ones_row", [1, 128], F32)
    c_ones_colb = alloc("c_ones_colb", [128, 1], BF16)
    c_ones_colf = alloc("c_ones_colf", [128, 1], F32)
    p_zbias = alloc("p_zbias", [128, 16], F32)
    p_convw = alloc("p_convw", [128, 192], F32)
    p_convb = alloc("p_convb", [128, 48], F32)
    p_dpair = alloc("p_dpair", [128, NPAIR], F32)
    p_rmsw = alloc("p_rmsw", [128, 16], F32)
    p_suug = alloc("p_suug", [128, 16], F32)
    p_sudp = alloc("p_sudp", [128, 64], F32)
    p_wsuup = alloc("p_wsuup", [128, 64], F32)
    p_wsugt = alloc("p_wsugt", [128, 64], F32)
    p_msel = alloc("p_msel", [8, 8], F32)
    p_agm = alloc("p_agm", [8, 1], F32)
    s_hsu16 = alloc("s_hsu16", [128, 16 * 128], BF16)
    s_hsu64 = alloc("s_hsu64", [128, 64 * 128], BF16)
    a_afm = alloc("a_afm", [32, 2048], F32)
    a_ascr = alloc("a_ascr", [32, 1080], F32)

    AB = alloc("AB", [128, 53856], BF16)
    AFp = alloc("AF", [128, 14348], F32)
    arb = Arena(53856)
    arf = Arena(14348)

    ctx = dict(nc=nc, P=P, E=E, buf=buf, pf=pf, PFB=PFB, arb=arb, arf=arf,
               AB=AB, AF=AFp, sbt=sbt, sems=sems, wsd_row_ap=wsd_row, dbg=ctxdbg, pf45=pf45)

    def ld(q, dst_ap, src_ap, wname, rds=()):
        E(q, lambda e, o=dst_ap, i=src_ap: e.dma_start(out=o, in_=i), rds, [wname])

    ld("Dw", c_ident_f[:, :], cident, "c_ident_f")
    ld("Dw", c_tril[:, :], ctril, "c_tril")
    ld("Dw", c_gmask2[:, :], cgmask2, "c_gmask2")
    ld("Dw", p_zbias[:, :], z_bias_fm, "p_zbias")
    ld("Dw", p_convw[:, :], conv_w_fm, "p_convw")
    ld("Dw", p_convb[:, :], conv_b_fm, "p_convb")
    ld("Dw", p_dpair[:, :], d_pair, "p_dpair")
    ld("Dw", p_rmsw[:, :], rms_w_fm, "p_rmsw")
    ld("Dw", p_suug[:, :], su_ug_fm, "p_suug")
    ld("Dw", p_sudp[:, :], su_dp_fm, "p_sudp")
    ld("Dw", p_msel[:, :], msel_t, "p_msel")
    ld("Dw", p_agm[:, :], agmask_i, "p_agm")
    ld("Dw", c_ones_row[:, :], cones_row, "c_ones_row")
    ld("Dw", c_h128f[:, :], ch128, "h128f")
    o_st = arf.take(256)
    for nm, src, dst, so in (("hblk16", chblk16, c_hblk16, 0),
                             ("hblk64", chblk64, c_hblk64, 128)):
        ld("Dw", AFp[:, o_st + so:o_st + so + 128], src, "gc_" + nm)
        E("V", lambda e, d=dst, so=so: ecopy(e, 
            d[:, :], AFp[:, o_st + so:o_st + so + 128]),
          ["gc_" + nm], [nm + "_b"])
    o_m8 = arf.take(128)
    ld("Dw", AFp[:, o_m8:o_m8 + 128], cm8, "gc_m8")
    E("V", lambda e, o_m8=o_m8: ecopy(e, c_m8p[:, :], AFp[:, o_m8:o_m8 + 128]),
      ["gc_m8"], ["c_m8p"])
    E("V", lambda e: e.tensor_scalar_mul(c_m8n[:, :], c_m8p[:, :], -1.0),
      ["c_m8p"], ["c_m8n"])
    o_sel = arf.take(2048)
    ld("Dw", AFp[0:32, o_sel:o_sel + 2048], cselpack, "gc_sel")
    E("V", lambda e: ecopy(e, c_sel[:, :], AFp[0:32, o_sel:o_sel + 2048]),
      ["gc_sel"], ["c_sel"])
    o_sel2 = arf.take(2048)
    ld("Dw", AFp[0:32, o_sel2:o_sel2 + 2048], csel2pack, "gc_sel2")
    E("V", lambda e: ecopy(e, c_sel2[:, :], AFp[0:32, o_sel2:o_sel2 + 2048]),
      ["gc_sel2"], ["c_sel2"])
    E("V", lambda e: e.memset(c_ones_colb[:, :], 1.0), [], ["c_ones_colb"])
    E("V", lambda e: e.memset(c_ones_colf[:, :], 1.0), [], ["c_ones_colf"])
    ld("Dw", p_wsuup[:, :], wsu_up_fm, "wsuup_raw")
    E("V", lambda e: e.tensor_scalar_mul(p_wsuup[:, :], p_wsuup[:, :], 64.0 * CORR16),
      ["wsuup_raw"], ["p_wsuup"])
    ld("Dw", p_wsugt[:, :], wsu_gate_fm, "wsugt_raw")
    E("V", lambda e: e.tensor_scalar_mul(p_wsugt[:, :], p_wsugt[:, :], 64.0 * CORR16),
      ["wsugt_raw"], ["p_wsugt"])
    for c in range(16):
        E("V", lambda e, c=c: e.tensor_scalar_mul(
            s_hsu16[:, c * 128:(c + 1) * 128], c_hblk16[:, :], p_suug[:, c:c + 1]),
          ["hblk16_b", "p_suug"], ["s_hsu16"])
    for uu in range(64):
        E("V", lambda e, u=uu: e.tensor_scalar_mul(
            s_hsu64[:, u * 128:(u + 1) * 128], c_hblk64[:, :], p_sudp[:, u:u + 1]),
          ["hblk64_b", "p_sudp"], ["s_hsu64"])

    P.barrier()
    arb.reset()
    arf.reset()
    o_ufm = arb.take(16 * JPAD)
    o_cvt = arb.take(2 * 2048)
    o_zst = arb.take(2 * 2048)
    o_wpan = arb.take(2 * 2048)
    o_xtmst = arb.take(2 * 2048)
    o_stage = arf.take(2 * JPAD)
    o_acc = arf.take(2 * 2048)
    o_wpanf = arf.take(2 * 2048)

    for c in range(UB_LIM if PHASES >= 1 else 0):
        h = c % 2
        E("Dw", lambda e, c=c, h=h: e.dma_start(
            out=AFp[:, o_stage + h * JPAD:o_stage + h * JPAD + 2048],
            in_=u_ext[c * 128:(c + 1) * 128, :]), [], [f"stg{h}"])
        for g in range(4):
            bank = pf[4 + (g % 2)]
            bname = PFB[4 + (g % 2)]

            def trg(e, h=h, g=g, bank=bank):
                inst = None
                for k in range(4):
                    dk = g * 4 + k
                    inst = e.transpose(
                        bank[:, k * 128:(k + 1) * 128],
                        AFp[:, o_stage + h * JPAD + dk * 128:
                            o_stage + h * JPAD + (dk + 1) * 128],
                        c_ident_f[:, :])
                return inst
            E("T", trg, [f"stg{h}", "c_ident_f"], [bname])

            def evu(e, c=c, g=g, bank=bank):
                w = min(128, 2051 - c * 128)
                inst = None
                for k in range(4):
                    dk = g * 4 + k
                    inst = ecopy(e, 
                        AB[:, o_ufm + dk * JPAD + c * 128:
                           o_ufm + dk * JPAD + c * 128 + w],
                        bank[:, k * 128:k * 128 + w])
                return inst
            E("V" if g % 2 == 0 else "S", evu, [bname], ["a_ufm"])

    def build_fold_steps(o_pf, o_pb, o_ev):
        loads, rests = [], []
        for p in range(64):
            hh = p % 2

            def panload(hh=hh, p=p):
                E("Dw", lambda e: e.dma_start(
                    out=AFp[:, o_pf + hh * 2048:o_pf + (hh + 1) * 2048],
                    in_=w_down_f[:, p * 2048:(p + 1) * 2048]), [], [f"wpf{hh}"])

            def panrest(hh=hh, p=p):
                ob = o_pb
                btok = "fsb0"
                for k in range(4):
                    kb = 4 + (k % 2)
                    E("T", lambda e, k=k, kb=kb: e.matmul(
                        pf[kb][:, :], sbt["c_h128f"][:, :],
                        AFp[:, o_pf + hh * 2048 + k * 512:
                            o_pf + hh * 2048 + (k + 1) * 512],
                        start=True, stop=True),
                      [f"wpf{hh}", "h128f"], [PFB[kb]])
                    E("V" if k % 2 == 0 else "S",
                      lambda e, k=k, kb=kb, ob=ob, btok=btok: ecopy(
                          e, AB[:, ob + k * 512:ob + (k + 1) * 512], pf[kb][:, :]),
                      [PFB[kb]], [btok])
                E("Ds", lambda e, ob=ob: e.dma_start(
                    out=wdnR[:, p * 2048:(p + 1) * 2048],
                    in_=AB[:, ob:ob + 2048]),
                  [btok], [f"fw2_{p}"])
            loads.append(panload)
            rests.append(panrest)
        return loads, rests

    ctx["build_fold_steps"] = build_fold_steps

    o_bev1 = arb.take(2048)
    o_bev2 = arb.take(2048)
    o_bsf = arf.take(1024)
    o_bgf = arf.take(1024)
    fold_loads, fold_rests = build_fold_steps(o_wpanf, o_bev1, o_bev2)

    bounce_steps = []
    bidx = [0]
    for ic in range(64):
        for wsrc, wdst, wi in ((w_up_p, wupB, 0), (w_gate_p, wgB, 1)):
            i = bidx[0]
            bidx[0] += 1
            pipe = i % 2
            dq, ce = ("Ds", "S") if pipe == 0 else ("Dp", "P")
            osf = o_bsf if pipe == 0 else o_bgf
            osb = o_bev2 + pipe * 1024
            ssrc = wsrc[:, ic * 1024:(ic + 1) * 1024]
            sdst = wdst[:, ic * 1024:(ic + 1) * 1024]

            def bstep(ssrc=ssrc, sdst=sdst, dq=dq, ce=ce, osf=osf,
                      osb=osb, pipe=pipe, wi=wi, ic=ic):
                E(dq, lambda e: e.dma_start(
                    out=AFp[:, osf:osf + 1024], in_=ssrc),
                  [], [f"bsf{pipe}"])
                E(ce, lambda e: ecopy(
                    e, AB[:, osb:osb + 1024], AFp[:, osf:osf + 1024]),
                  [f"bsf{pipe}"], [f"bsb{pipe}"])
                E(dq, lambda e: e.dma_start(
                    out=sdst, in_=AB[:, osb:osb + 1024]),
                  [f"bsb{pipe}"], [f"fw{wi}_{ic}"])
            bounce_steps.append(bstep)

    def make_ssd_bounce(osf0, osf1, osb0, osb1):
        steps = []
        k = [0]
        for ic in range(64, 128):
            for half in range(2):
                for wsrc, wdst, wi in ((w_up_p, wupB, 0), (w_gate_p, wgB, 1)):
                    i = k[0]
                    k[0] += 1
                    pipe = i % 2
                    dq, ce = ("Ds", "S") if pipe == 0 else ("Dp", "P")
                    osf = osf0 if pipe == 0 else osf1
                    osb = osb0 if pipe == 0 else osb1
                    c0 = ic * 1024 + half * 512
                    ssrc = wsrc[:, c0:c0 + 512]
                    sdst = wdst[:, c0:c0 + 512]

                    def bstep(ssrc=ssrc, sdst=sdst, dq=dq, ce=ce, osf=osf,
                              osb=osb, pipe=pipe, wi=wi, ic=ic, half=half):
                        E(dq, lambda e: e.dma_start(
                            out=AFp[:, osf:osf + 512], in_=ssrc),
                          [], [f"Bsf{pipe}"])
                        E(ce, lambda e: ecopy(
                            e, AB[:, osb:osb + 512], AFp[:, osf:osf + 512]),
                          [f"Bsf{pipe}"], [f"Bsb{pipe}"])
                        E(dq, lambda e: e.dma_start(
                            out=sdst, in_=AB[:, osb:osb + 512]),
                          [f"Bsb{pipe}"], [f"fw{wi}_{ic}_{half}"])
                    steps.append(bstep)
        return steps
    ctx["make_ssd_bounce"] = make_ssd_bounce

    def drain_bounce(n):
        for _ in range(n):
            if bounce_steps:
                bounce_steps.pop(0)()
    fold_state = [0]

    def drain_fold(n):
        for _ in range(n):
            i = fold_state[0]
            if i > 64:
                return
            if i < 64:
                fold_loads[i]()
            if i >= 1:
                fold_rests[i - 1]()
            fold_state[0] = i + 1
    ctx["drain_fold"] = drain_fold

    tblocks = [(0, 512), (512, 512), (1024, 512), (1536, 512), (2048, 3)]

    def emit_proj_head(cc):
        h = cc % 2
        src = w_in_p[:, cc * 2048:(cc + 1) * 2048]
        E("Dw", lambda e, h=h, src=src: e.dma_start(
            out=AFp[:, o_wpanf + h * 2048:o_wpanf + h * 2048 + 2048], in_=src),
          [], [f"wpf{h}"])
        E("V",
          lambda e, h=h: ecopy(e,
              AB[:, o_wpan + h * 2048:o_wpan + h * 2048 + 2048],
              AFp[:, o_wpanf + h * 2048:o_wpanf + h * 2048 + 2048]),
          [f"wpf{h}"], [f"wpan{h}"])
        drain_fold(1)
        drain_bounce(2)

    def emit_proj(cc):
        h = cc % 2
        for tb, (j0, N) in enumerate(tblocks):
            bank = pf[tb % 2]
            bname = PFB[tb % 2]

            def do_mm(e, h=h, j0=j0, N=N, bank=bank):
                inst = None
                for dk in range(16):
                    inst = e.matmul(
                        bank[0:128, 0:N],
                        AB[:, o_wpan + h * 2048 + dk * 128:
                           o_wpan + h * 2048 + (dk + 1) * 128],
                        AB[:, o_ufm + dk * JPAD + j0:o_ufm + dk * JPAD + j0 + N],
                        start=(dk == 0), stop=(dk == 15))
                return inst
            E("T", do_mm, [f"wpan{h}", "a_ufm"], [bname])
            if cc < 48:
                E("S",
                  lambda e, h=h, j0=j0, N=N, bank=bank: ecopy(e,
                      AFp[:, o_stage + h * JPAD + j0:o_stage + h * JPAD + j0 + N],
                      bank[0:128, 0:N]),
                  [bname], [f"stg{h}"])
            else:
                lo = max(j0, 3)
                hi = j0 + N
                if hi <= 3:
                    continue
                if cc < 64:
                    zc = cc - 48
                    E("S", lambda e, h=h, zc=zc, lo=lo, hi=hi, j0=j0, bank=bank:
                      e.activation(
                          AB[:, o_zst + h * 2048 + lo - 3:o_zst + h * 2048 + hi - 3],
                          bank[0:128, lo - j0:hi - j0], AF.Silu,
                          bias=p_zbias[:, zc:zc + 1], scale=1.0),
                      [bname, "p_zbias"], [f"zst{h}"])
                else:
                    n = hi - lo
                    sA = bank[0:32, lo - j0:hi - j0]
                    E("S", lambda e, sA=sA, n=n: e.activation(
                        a_ascr[0:32, 0:n], sA, AF.Relu), [bname], ["Ar"])
                    E("S", lambda e, sA=sA, n=n: e.activation(
                        a_ascr[0:32, 550:550 + n], sA, AF.Abs), [bname], ["Aa"])
                    E("S", lambda e, n=n: e.activation(
                        a_ascr[0:32, 550:550 + n], a_ascr[0:32, 550:550 + n],
                        AF.Exp, scale=-1.0), ["Aa"], ["Aa"])
                    E("S", lambda e, n=n: e.activation(
                        a_ascr[0:32, 550:550 + n], a_ascr[0:32, 550:550 + n],
                        AF.Ln, bias=1.0), ["Aa"], ["Aa"])
                    E("V", lambda e, lo=lo, hi=hi, n=n: e.scalar_tensor_tensor(
                        a_afm[0:32, lo - 3:hi - 3], a_ascr[0:32, 0:n], -1.0,
                        a_ascr[0:32, 550:550 + n], ALU.mult, ALU.subtract),
                      ["Ar", "Aa"], ["a_afm"])
    def emit_conv_compute(cc):
        h = cc % 2
        if cc < 48:
            sg = o_stage + h * JPAD
            aA = AFp[:, o_acc + h * 2048:o_acc + (h + 1) * 2048]
            aB = AFp[:, o_acc + (1 - h) * 2048:o_acc + (2 - h) * 2048]
            E("V", lambda e, sg=sg, aA=aA, cc=cc: e.tensor_scalar(
                aA, AFp[:, sg:sg + 2048], p_convw[:, cc * 4:cc * 4 + 1],
                p_convb[:, cc:cc + 1], ALU.mult, ALU.add),
              [f"stg{h}", "p_convw", "p_convb"], [f"acch{h}"])
            E("V", lambda e, sg=sg, aA=aA, aB=aB, cc=cc: e.scalar_tensor_tensor(
                aB, AFp[:, sg + 1:sg + 2049], p_convw[:, cc * 4 + 1:cc * 4 + 2],
                aA, ALU.mult, ALU.add),
              [f"stg{h}", f"acch{h}", "p_convw"], [f"acch{1-h}"])
            E("V", lambda e, sg=sg, aA=aA, aB=aB, cc=cc: e.scalar_tensor_tensor(
                aA, AFp[:, sg + 2:sg + 2050], p_convw[:, cc * 4 + 2:cc * 4 + 3],
                aB, ALU.mult, ALU.add),
              [f"stg{h}", f"acch{1-h}", "p_convw"], [f"acch{h}"])
            E("V", lambda e, sg=sg, aA=aA, aB=aB, cc=cc: e.scalar_tensor_tensor(
                aB, AFp[:, sg + 3:sg + 2051], p_convw[:, cc * 4 + 3:cc * 4 + 4],
                aA, ALU.mult, ALU.add),
              [f"stg{h}", f"acch{h}", "p_convw"], [f"acch{1-h}"])

    def emit_conv_tail(cc):
        h = cc % 2
        if cc < 48:
            aB = AFp[:, o_acc + (1 - h) * 2048:o_acc + (2 - h) * 2048]
            E("S", lambda e, aB=aB, h=h: e.activation(
                AB[:, o_cvt + h * 2048:o_cvt + (h + 1) * 2048], aB, AF.Silu),
              [f"acch{1-h}"], [f"cvo{h}"])
            E("Ds", lambda e, cc=cc, h=h: e.dma_start(
                out=sp_conv[cc], in_=AB[:, o_cvt + h * 2048:o_cvt + (h + 1) * 2048]),
              [f"cvo{h}"], [f"spc{cc}"])
        elif cc < 64:
            E("Ds", lambda e, cc=cc, h=h: e.dma_start(
                out=sp_z[cc - 48], in_=AB[:, o_zst + h * 2048:o_zst + (h + 1) * 2048]),
              [f"zst{h}"], [f"spz{cc-48}"])

    def emit_conv_trans(cc):
        h = cc % 2
        if cc < 32:
            aB = AFp[:, o_acc + (1 - h) * 2048:o_acc + (2 - h) * 2048]
            if True:
                for g in range(4):
                    bank2 = pf[4 + (g % 2)]
                    bname2 = PFB[4 + (g % 2)]

                    def trx(e, aB=aB, g=g, bank2=bank2):
                        inst = None
                        for k in range(4):
                            tcx = g * 4 + k
                            inst = e.transpose(
                                bank2[:, k * 128:(k + 1) * 128],
                                aB[:, tcx * 128:(tcx + 1) * 128],
                                c_ident_f[:, :])
                        return inst
                    E("T", trx, [f"acch{1-h}", "c_ident_f"], [bname2])
                    E("S", lambda e, h=h, g=g, bank2=bank2: e.activation(
                        AB[:, o_xtmst + h * 2048 + g * 512:
                           o_xtmst + h * 2048 + (g + 1) * 512],
                        bank2[:, 0:512], AF.Silu), [bname2], [f"xts{h}"])
                E("Ds", lambda e, cc=cc, h=h: e.dma_start(
                    out=sp_xbtm[cc].rearrange("c p t -> p c t"),
                    in_=AB[:, o_xtmst + h * 2048:o_xtmst + (h + 1) * 2048]),
                  [f"xts{h}"], [f"spt{cc}"])

    if PHASES >= 2:
        emit_proj_head(0)
    for cc in range(65 if PHASES >= 2 else 0):
        if cc + 1 < 65:
            emit_proj_head(cc + 1)
        if cc >= 1:
            emit_conv_compute(cc - 1)
        emit_proj(cc)
        if cc >= 1:
            emit_conv_tail(cc - 1)
            emit_conv_trans(cc - 1)

    drain_fold(1000)
    drain_bounce(1000)

    if PHASES >= 4:
        emit_ssd(ctx, sp_conv, sp_xbtm, sp_y, sp_v, agin)
    if PHASES >= 5:
        emit_combine(ctx, agin, agout, w_out_t)
    if PHASES >= 6:
        emit_p3(ctx, w_out_t, u_ext, sp_y, sp_v, sp_z, sp_x1, sp_x0)
    if PHASES >= 7:
        emit_p4(ctx, wupB, wgB, wdnR, sp_x2, sp_x0, sp_x1, out)
    else:
        P.barrier()
        E("V", lambda e: e.memset(AFp[:, 0:2048], 0.0), [], ["stubz"])
        for c in range(16):
            E("Dp", lambda e, c=c: e.dma_start(
                out=out[c * 128:(c + 1) * 128, :], in_=AFp[:, 0:2048]),
              ["stubz"], [f"outc{c}"])

    P.final_waits()
    with nc.Block() as block:
        P.realize(block, sems)
    stack.close()
    return nc


def emit_ssd(ctx, sp_conv, sp_xbtm, sp_y, sp_v, agin):
    P, E, pf, PFB = ctx["P"], ctx["E"], ctx["pf"], ctx["PFB"]
    arb, arf, AB, AFp, sbt = ctx["arb"], ctx["arf"], ctx["AB"], ctx["AF"], ctx["sbt"]
    c_tril = sbt["c_tril"]
    c_ident_f = sbt["c_ident_f"]
    c_sel = sbt["c_sel"]
    c_sel2 = sbt["c_sel2"]
    a_afm = sbt["a_afm"]
    a_ascr = sbt["a_ascr"]
    p_dpair = sbt["p_dpair"]
    pf45 = ctx["pf45"]

    P.barrier()
    arb.reset()
    arf.reset()
    o_slx = arb.take(2 * 2048)
    o_bsl = arb.take(2 * 4096)
    o_csl = arb.take(2 * 4096)
    o_xtm = arb.take(2 * 2048)
    o_btm = arb.take(2 * 2048)
    o_xtme = arb.take(2 * 2048)
    o_gm = arb.take(2 * 256)
    o_eb = arb.take(2 * 2048)
    o_ebghm = arb.take(4096)
    o_sprefs = arb.take(2048)
    o_yst = arb.take(2 * 2048)
    o_vst = arb.take(4096)
    o_ehsb = arb.take(256)
    o_ex3b = arb.take(16)
    o_einvb = arb.take(2 * 128)
    o_wbs = arb.take(2 * 64)
    o_wsc = arb.take(2 * 128)
    o_ones64 = arb.take(64)
    o_srun = arf.take(2048)
    o_schunk = arf.take(2048)
    o_tsc2 = arf.take(2048)
    o_sbf0 = arf.take(512)
    o_sbf1 = arf.take(512)
    o_sbb = arb.take(2 * 512)
    ssd_bounce = ctx["make_ssd_bounce"](o_sbf0, o_sbf1, o_sbb, o_sbb + 512)
    o_tscan = arf.take(64)
    o_ytmp = arf.take(512)
    o_acol = arf.take(128)
    o_scolh = arf.take(2 * 96)

    E("V", lambda e: e.memset(AB[64:128, o_bsl:o_bsl + 2 * 4096], 0.0),
      [], ["bslpad"])
    E("V", lambda e: e.memset(AB[64:128, o_csl:o_csl + 2 * 4096], 0.0),
      [], ["cslpad"])
    E("V", lambda e: e.memset(AB[64:128, o_sprefs:o_sprefs + 2048], 0.0),
      [], ["sprefspad"])
    E("V", lambda e: e.memset(AFp[0:64, o_srun:o_srun + 2048], 0.0), [], ["srun"])
    E("V", lambda e: e.memset(a_ascr[0:32, 512:513], 0.0), [], ["RD"])
    E("V", lambda e: e.memset(AB[0:32, o_ones64:o_ones64 + 64], 1.0),
      [], ["ones64"])

    def emit_headA(c):
        h = c % 2
        slb0 = o_slx + h * 2048
        E("Dw", lambda e, c=c, slb0=slb0: e.dma_start(
            out=AB[:, slb0:slb0 + 2048],
            in_=sp_conv[0:16, :, c * 128:(c + 1) * 128].rearrange(
                "a p t -> p a t")),
          [f"spc{cc}" for cc in range(16)], [f"sl{h}"])
        E("Dw", lambda e, c=c, h=h: e.dma_start(
            out=AB[0:64, o_bsl + h * 4096:o_bsl + (h + 1) * 4096],
            in_=sp_conv[16:32, :, c * 128:(c + 1) * 128].rearrange(
                "j (e n) t -> n (j e) t", e=2, n=64)),
          [f"spc{cc}" for cc in range(16, 32)], [f"bsl{h}"])
        E("Dw", lambda e, c=c, h=h: e.dma_start(
            out=AB[0:64, o_csl + h * 4096:o_csl + (h + 1) * 4096],
            in_=sp_conv[32:48, :, c * 128:(c + 1) * 128].rearrange(
                "j (e n) t -> n (j e) t", e=2, n=64)),
          [f"spc{cc}" for cc in range(32, 48)], [f"csl{h}"])
        E("Dw", lambda e, c=c, h=h: e.dma_start(
            out=AB[:, o_xtm + h * 2048:o_xtm + (h + 1) * 2048],
            in_=sp_xbtm[0:16, c].rearrange("a p t -> p a t")),
          [f"spt{cc}" for cc in range(16)], [f"xtm{h}"])
        E("Dw", lambda e, c=c, h=h: e.dma_start(
            out=AB[:, o_btm + h * 2048:o_btm + (h + 1) * 2048],
            in_=sp_xbtm[16:32, c].rearrange("a p t -> p a t")),
          [f"spt{cc+16}" for cc in range(16)], [f"btm{h}"])

        E("T", lambda e, c=c: e.transpose(
            pf[6][:, 0:32], a_afm[0:32, c * 128:(c + 1) * 128],
            c_ident_f[0:32, 0:32]),
          ["a_afm", "c_ident_f"], [PFB[6]])
        E("V", lambda e: ecopy(e, AFp[:, o_acol:o_acol + 32], pf[6][:, 0:32]),
          [PFB[6]], ["At"])
        E("T", lambda e: e.matmul(pf[4][:, 384:416], c_tril[:, :],
                                  AFp[:, o_acol:o_acol + 32], start=True, stop=True),
          ["At", "c_tril"], [PFB[4]])
        E("V", lambda e: ecopy(e, AFp[:, o_acol + 32:o_acol + 64],
                               pf[4][:, 384:416]), [PFB[4]], ["CS"])
        E("T", lambda e: e.transpose(pf[5][0:32, 384:512],
                                     AFp[:, o_acol + 32:o_acol + 64],
                                     c_ident_f[:, :]),
          ["CS", "c_ident_f"], [PFB[5]])
        E("V", lambda e: ecopy(e, a_ascr[0:32, 0:128], pf[5][0:32, 384:512]),
          [PFB[5]], ["CS_hs"])
        E("V", lambda e: e.tensor_scalar(
            a_ascr[0:32, 128:256], a_ascr[0:32, 0:128], a_ascr[0:32, 64:65], None,
            ALU.subtract), ["CS_hs"], ["CSm"])
        E("S", lambda e: e.activation(a_ascr[0:32, 256:384], a_ascr[0:32, 128:256],
                                      AF.Exp), ["CSm"], ["E_hs"])
        E("S", lambda e: e.activation(a_ascr[0:32, 384:512], a_ascr[0:32, 128:256],
                                      AF.Exp, scale=-1.0), ["CSm"], ["Einv_hs"])
        E("V", lambda e: ecopy(e, AB[0:32, o_ehsb:o_ehsb + 128],
                               a_ascr[0:32, 256:384]), ["E_hs"], ["E_hsb"])
        E("V", lambda e: e.tensor_tensor(a_ascr[0:32, 524:525],
                                         a_ascr[0:32, 64:65],
                                         a_ascr[0:32, 512:513], ALU.add),
          ["CS_hs", "RD"], ["mRD"])
        E("S", lambda e: e.activation(a_ascr[0:32, 525:526], a_ascr[0:32, 524:525],
                                      AF.Exp), ["mRD"], ["expmRD"])
        E("V", lambda e: e.tensor_scalar(
            AB[0:32, o_ehsb + 128:o_ehsb + 256], a_ascr[0:32, 256:384],
            a_ascr[0:32, 525:526], None, ALU.mult),
          ["E_hs", "expmRD"], ["Eg_hsb"])
        E("V", lambda e: e.tensor_tensor(a_ascr[0:32, 512:513],
                                         a_ascr[0:32, 512:513],
                                         a_ascr[0:32, 127:128], ALU.add),
          ["CS_hs", "RD"], ["RD"])
        E("V", lambda e: ecopy(e, a_ascr[0:32, 520:521],
                               a_ascr[0:32, 127:128]), ["CS_hs"], ["rhs3"])
        E("V", lambda e: e.tensor_scalar(
            a_ascr[0:32, 521:522], a_ascr[0:32, 127:128], a_ascr[0:32, 64:65],
            None, ALU.subtract), ["CS_hs"], ["rhs3"])
        E("V", lambda e: ecopy(e, a_ascr[0:32, 522:523],
                               a_ascr[0:32, 64:65]), ["CS_hs"], ["rhs3"])
        E("S", lambda e: e.activation(a_ascr[0:32, 530:533], a_ascr[0:32, 520:523],
                                      AF.Exp), ["rhs3"], ["exp3"])
        E("V", lambda e, h=h: ecopy(e, AB[0:32, o_einvb + h * 128:
                                          o_einvb + h * 128 + 128],
                                    a_ascr[0:32, 384:512]),
          ["Einv_hs"], [f"einvb{h}"])
        E("V", lambda e, h=h: e.tensor_scalar(
            AB[0:32, o_wbs + h * 64:o_wbs + (h + 1) * 64],
            AB[0:32, o_ones64:o_ones64 + 64],
            a_ascr[0:32, 532:533], None, ALU.mult),
          ["ones64", "exp3"], [f"wbs{h}"])
        E("V", lambda e, h=h: e.tensor_scalar(
            AB[0:32, o_wsc + h * 128:o_wsc + h * 128 + 64],
            AB[0:32, o_ones64:o_ones64 + 64],
            a_ascr[0:32, 530:531], None, ALU.mult),
          ["ones64", "exp3"], [f"wsc0{h}"])
        E("V", lambda e, h=h: e.tensor_scalar(
            AB[0:32, o_wsc + h * 128 + 64:o_wsc + h * 128 + 128],
            AB[0:32, o_ones64:o_ones64 + 64],
            a_ascr[0:32, 531:532], None, ALU.mult),
          ["ones64", "exp3"], [f"wsc1{h}"])

    def emit_headB(c):
        h = c % 2
        for g in range(4):
            bank = pf[6 + (g % 2)]
            bname = PFB[6 + (g % 2)]

            def ebmm(e, g=g, bank=bank):
                inst = None
                for q in range(4):
                    jj = g * 4 + q
                    inst = e.matmul(bank[:, q * 128:(q + 1) * 128],
                                    c_sel[:, jj * 128:(jj + 1) * 128],
                                    AB[0:32, o_ehsb:o_ehsb + 128],
                                    start=True, stop=True)
                return inst
            E("T", ebmm, ["E_hsb", "c_sel"], [bname])
            E("V", lambda e, g=g, h=h, bank=bank: ecopy(e,
                AB[:, o_eb + h * 2048 + g * 512:o_eb + h * 2048 + (g + 1) * 512],
                bank[:, 0:512]),
              [bname], [f"eb{h}"])
        for g in range(8):
            bank = pf[6 + (g % 2)]
            bname = PFB[6 + (g % 2)]

            def ebgmm(e, g=g, bank=bank):
                inst = None
                for q in range(4):
                    hh = g * 4 + q
                    inst = e.matmul(bank[0:64, q * 128:(q + 1) * 128],
                                    c_sel2[:, hh * 64:(hh + 1) * 64],
                                    AB[0:32, o_ehsb + 128:o_ehsb + 256],
                                    start=True, stop=True)
                return inst
            E("T", ebgmm, ["Eg_hsb", "c_sel2"], [bname])
            E("S", lambda e, g=g, bank=bank: e.copy(
                AB[0:64, o_ebghm + g * 512:o_ebghm + (g + 1) * 512],
                bank[0:64, 0:512]),
              [bname], ["ebghm"])
        for g in range(4):
            bank = pf[6 + (g % 2)]
            bname = PFB[6 + (g % 2)]
            E("T", lambda e, g=g, h=h, bank=bank: e.matmul(
                bank[:, 0:512],
                AB[0:32, o_einvb + h * 128:o_einvb + h * 128 + 128],
                c_sel[:, g * 512:(g + 1) * 512], start=True, stop=True),
              [f"einvb{h}", "c_sel"], [bname])
            E("V", lambda e, g=g, h=h, bank=bank: e.tensor_tensor(
                AB[:, o_xtme + h * 2048 + g * 512:
                   o_xtme + h * 2048 + (g + 1) * 512],
                AB[:, o_xtm + h * 2048 + g * 512:
                   o_xtm + h * 2048 + (g + 1) * 512],
                bank[:, 0:512], ALU.mult),
              [f"xtm{h}", bname], [f"xtme{h}"])
        E("P", lambda e, h=h: e.tensor_tensor(
            AB[0:64, o_vst:o_vst + 4096],
            AB[0:64, o_csl + h * 4096:o_csl + (h + 1) * 4096],
            AB[0:64, o_ebghm:o_ebghm + 4096], ALU.mult),
          [f"csl{h}", "ebghm"], ["vst"])
        E("Dw", lambda e, c=c: e.dma_start(
            out=sp_v[c], in_=AB[0:64, o_vst:o_vst + 4096]),
          ["vst"], [f"spv{c}"])

    def emit_sprefs(c):
        h = c % 2
        for g in range(4):
            bank = pf[4 + (g % 2)]
            bname = PFB[4 + (g % 2)]
            E("T", lambda e, g=g, h=h, bank=bank: e.matmul(
                bank[0:64, 0:512],
                AB[0:32, o_wbs + h * 64:o_wbs + (h + 1) * 64],
                c_sel2[:, g * 512:(g + 1) * 512], start=True, stop=True),
              [f"wbs{h}", "c_sel2"], [bname])
            E("V", lambda e, g=g, bank=bank: e.tensor_tensor(
                AB[0:64, o_sprefs + g * 512:o_sprefs + (g + 1) * 512],
                AFp[0:64, o_srun + g * 512:o_srun + (g + 1) * 512],
                bank[0:64, 0:512], ALU.mult),
              ["srun", bname], ["sprefs"])

    def emit_pairs(c, j0, j1):
        h = c % 2
        slb0 = o_slx + h * 2048

        def emit_gp(jj):
            gbank = pf[0 + (jj % 2)]
            gbname = PFB[0 + (jj % 2)]

            def gmm(e, jj=jj, h=h, gbank=gbank):
                inst = None
                for e_ in range(2):
                    hh = 2 * jj + e_
                    inst = e.matmul(
                        gbank[:, e_ * 128:(e_ + 1) * 128],
                        AB[:, o_bsl + h * 4096 + hh * 128:
                           o_bsl + h * 4096 + (hh + 1) * 128],
                        AB[:, o_csl + h * 4096 + hh * 128:
                           o_csl + h * 4096 + (hh + 1) * 128],
                        start=True, stop=True)
                return inst
            E("T", gmm, [f"bsl{h}", f"csl{h}", "bslpad", "cslpad"], [gbname])

        def emit_fin(jj):
            gbank = pf[0 + (jj % 2)]
            gbname = PFB[0 + (jj % 2)]
            g = jj // 4
            q = jj % 4
            ybank = pf[2 + (g % 2)]
            ybname = PFB[2 + (g % 2)]
            gmo = o_gm + (jj % 2) * 256
            E("V", lambda e, gmo=gmo, gbank=gbank: e.tensor_tensor(
                AB[:, gmo:gmo + 256], gbank[:, 0:256],
                sbt["c_gmask2"][:, :], ALU.mult),
              [gbname, "c_gmask2"], [f"gm{jj%2}"])

            def ymm(e, jj=jj, h=h, ybank=ybank, q=q, gmo=gmo):
                inst = None
                for e_ in range(2):
                    hh = 2 * jj + e_
                    xe = AB[:, o_xtme + h * 2048 + jj * 128 + e_ * 64:
                            o_xtme + h * 2048 + jj * 128 + e_ * 64 + 64]
                    inst = e.matmul(
                        ybank[e_ * 64:(e_ + 1) * 64, q * 128:(q + 1) * 128], xe,
                        AB[:, gmo + e_ * 128:gmo + (e_ + 1) * 128],
                        start=True, stop=False)
                    sp = AB[:, o_sprefs + hh * 64:o_sprefs + (hh + 1) * 64]
                    cc_ = AB[:, o_csl + h * 4096 + hh * 128:
                             o_csl + h * 4096 + (hh + 1) * 128]
                    inst = e.matmul(
                        ybank[e_ * 64:(e_ + 1) * 64, q * 128:(q + 1) * 128], sp,
                        cc_, start=False, stop=True)
                return inst
            E("T", ymm, [f"xtme{h}", f"gm{jj%2}", "sprefs", "sprefspad", f"csl{h}",
                         "cslpad"], [ybname])

            def smm(e, jj=jj, h=h):
                inst = None
                for e_ in range(2):
                    bt = AB[:, o_btm + h * 2048 + jj * 128 + e_ * 64:
                            o_btm + h * 2048 + jj * 128 + e_ * 64 + 64]
                    xe = AB[:, o_xtme + h * 2048 + jj * 128 + e_ * 64:
                            o_xtme + h * 2048 + jj * 128 + e_ * 64 + 64]
                    inst = e.matmul(
                        pf[0 + (jj % 2)][0:64, 256 + e_ * 64:256 + (e_ + 1) * 64],
                        bt, xe, start=True, stop=True)
                return inst
            E("T", smm, [f"btm{h}", f"xtme{h}"], [PFB[0 + (jj % 2)]])
            E("S", lambda e, jj=jj: ecopy(e,
                AFp[0:64, o_schunk + jj * 128:o_schunk + (jj + 1) * 128],
                pf[0 + (jj % 2)][0:64, 256:384]),
              [PFB[0 + (jj % 2)]], ["schunk"])
            if q == 3:
                E("V", lambda e, g=g, h=h, ybank=ybank: e.tensor_tensor(
                    AFp[:, o_ytmp:o_ytmp + 512], ybank[:, 0:512],
                    AB[:, o_eb + h * 2048 + g * 512:o_eb + h * 2048 + (g + 1) * 512],
                    ALU.mult),
                  [ybname, f"eb{h}"], ["ytmp"])
                for q2 in range(4):
                    jj2 = g * 4 + q2
                    xpair = AB[:, slb0 + jj2 * 128:slb0 + (jj2 + 1) * 128]
                    E("V", lambda e, jj2=jj2, h=h, xpair=xpair, q2=q2:
                      e.scalar_tensor_tensor(
                        AB[:, o_yst + h * 2048 + jj2 * 128:
                           o_yst + h * 2048 + (jj2 + 1) * 128],
                        xpair, p_dpair[:, jj2:jj2 + 1],
                        AFp[:, o_ytmp + q2 * 128:o_ytmp + (q2 + 1) * 128],
                        ALU.mult, ALU.add),
                      [f"sl{h}", "p_dpair", "ytmp"], [f"yst{h}"])

        for jj in range(j0, j1):
            emit_gp(jj)
            if jj >= 1:
                emit_fin(jj - 1)
        if j1 == NPAIR:
            emit_fin(NPAIR - 1)
            E("Dw", lambda e, c=c, h=h: e.dma_start(
                out=sp_y[c],
                in_=AB[:, o_yst + h * 2048:o_yst + (h + 1) * 2048]),
              [f"yst{h}"], [f"spy{c}"])
            for g in range(4):
                bank = pf[4 + g]
                bname = PFB[4 + g]
                E("T", lambda e, g=g, h=h, bank=bank: e.matmul(
                    bank[0:64, 0:512],
                    AB[0:32, o_wsc + h * 128:o_wsc + h * 128 + 64],
                    c_sel2[:, g * 512:(g + 1) * 512], start=True, stop=True),
                  [f"wsc0{h}", "c_sel2"], [bname])
                E("T", lambda e, g=g, h=h, bank=bank: e.matmul(
                    bank[64:128, 0:512],
                    AB[0:32, o_wsc + h * 128 + 64:o_wsc + h * 128 + 128],
                    c_sel2[:, g * 512:(g + 1) * 512], start=True, stop=True),
                  [f"wsc1{h}", "c_sel2"], [bname])
                E("V", lambda e, g=g, bank=bank: e.tensor_tensor(
                    AFp[0:64, o_srun + g * 512:o_srun + (g + 1) * 512],
                    AFp[0:64, o_srun + g * 512:o_srun + (g + 1) * 512],
                    bank[0:64, 0:512], ALU.mult),
                  ["srun", bname], ["srun"])
                E("V", lambda e, g=g, bank=bank: e.tensor_tensor(
                    AFp[0:64, o_tsc2 + g * 512:o_tsc2 + (g + 1) * 512],
                    AFp[0:64, o_schunk + g * 512:o_schunk + (g + 1) * 512],
                    bank[64:128, 0:512], ALU.mult),
                  ["schunk", bname], ["tsc2"])
                E("V", lambda e, g=g: e.tensor_tensor(
                    AFp[0:64, o_srun + g * 512:o_srun + (g + 1) * 512],
                    AFp[0:64, o_srun + g * 512:o_srun + (g + 1) * 512],
                    AFp[0:64, o_tsc2 + g * 512:o_tsc2 + (g + 1) * 512],
                    ALU.add), ["srun", "tsc2"], ["srun"])

    def drain_sb(n):
        for _ in range(n):
            if ssd_bounce:
                ssd_bounce.pop(0)()
    if SSD_LIM > 0:
        emit_headA(0)
        emit_headB(0)
    for c in range(SSD_LIM):
        emit_sprefs(c)
        drain_sb(8)
        if c + 1 < SSD_LIM:
            emit_headA(c + 1)
        emit_pairs(c, 0, 8)
        drain_sb(8)
        if c + 1 < SSD_LIM:
            emit_headB(c + 1)
        emit_pairs(c, 8, NPAIR)
    drain_sb(1000)

    E("V", lambda e: e.memset(AFp[0:64, o_tscan:o_tscan + 16], 0.0),
      ["tscan"], ["zpad"])
    E("Ds", lambda e: e.dma_start(out=agin[0:64, 2048:2056],
                                  in_=AFp[0:64, o_tscan:o_tscan + 8]),
      ["zpad"], ["agin"])
    E("Ds", lambda e: e.dma_start(out=agin[0:64, 0:2048],
                                  in_=AFp[0:64, o_srun:o_srun + 2048]),
      ["srun"], ["agin"])
    E("Ds", lambda e: e.dma_start(out=agin[0:32, 2048:2049],
                                  in_=a_ascr[0:32, 512:513],
                                  allow_slow_non_contiguous=True),
      ["RD"], ["agin"])

def emit_combine(ctx, agin, agout, woutbf):
    P, E, pf, PFB = ctx["P"], ctx["E"], ctx["pf"], ctx["PFB"]
    arb, arf, AB, AFp, sbt = ctx["arb"], ctx["arf"], ctx["AB"], ctx["AF"], ctx["sbt"]
    c_sel2 = sbt["c_sel2"]
    c_ident_f = sbt["c_ident_f"]
    p_msel = sbt["p_msel"]
    p_agm = sbt["p_agm"]
    a_ascr = sbt["a_ascr"]

    P.barrier()
    arb.reset()
    arf.reset()
    o_sinb = arb.take(2048 + 64)
    arb.take(64)
    o_woutc = arb.take(16 * 2048)
    o_ones64c = arb.take(64)
    o_wbc = arb.take(64)
    o_sj = arf.take(3 * 2048)
    o_tmp = arf.take(2048)
    o_sinA = arf.take(2048)
    o_wstg = arf.take(2 * 2048)
    assert o_woutc == 2176

    E("CC", lambda e: e.collective_compute(
        "AllGather", ALU.bypass, replica_groups=[list(range(NCORES))],
        ins=[agin.opt()], outs=[agout.opt()]),
      ["agin"], ["agout"])
    for ec in range(16):
        h2 = ec % 2
        E("Dw", lambda e, ec=ec, h2=h2: e.dma_start(
            out=AFp[:, o_wstg + h2 * 2048:o_wstg + (h2 + 1) * 2048],
            in_=woutbf[ec * 128:(ec + 1) * 128, :]), [], [f"wst{h2}"])
        E("V" if ec % 2 == 0 else "S",
          lambda e, ec=ec, h2=h2: ecopy(e,
              AB[:, o_woutc + ec * 2048:o_woutc + (ec + 1) * 2048],
              AFp[:, o_wstg + h2 * 2048:o_wstg + (h2 + 1) * 2048]),
          [f"wst{h2}"], ["woutres"])
    src = bass.AP(agout.tensor, 2048, [[64 * 2056, 8], [2056, 32]])
    E("Ds", lambda e, src=src: e.dma_start(
        out=a_ascr[0:8, 600:632], in_=src, allow_slow_non_contiguous=True),
      ["agout"], ["RDallT"])
    E("T", lambda e: e.matmul(pf[6][0:8, 0:32], p_msel[:, :], a_ascr[0:8, 600:632],
                              start=True, stop=True),
      ["RDallT", "p_msel"], [PFB[6]])
    E("S", lambda e: e.activation(a_ascr[0:8, 640:672], pf[6][0:8, 0:32], AF.Exp),
      [PFB[6]], ["expw"])
    E("V", lambda e: e.tensor_scalar(a_ascr[0:8, 640:672], a_ascr[0:8, 640:672],
                                     p_agm[:, 0:1], None, ALU.mult),
      ["expw", "p_agm"], ["expw"])
    E("T", lambda e: e.transpose(pf[6][0:32, 32:40], a_ascr[0:8, 640:672],
                                 c_ident_f[0:8, 0:8]),
      ["expw", "c_ident_f"], [PFB[6]])
    E("V", lambda e: ecopy(e, a_ascr[0:32, 700:708], pf[6][0:32, 32:40]),
      [PFB[6]], ["wT"])
    E("V", lambda e: e.memset(AB[0:32, o_ones64c:o_ones64c + 64], 1.0),
      [], ["ones64c"])

    def ldj(j):
        sl = j % 3
        E("Ds", lambda e, j=j, sl=sl: e.dma_start(
            out=AFp[0:64, o_sj + sl * 2048:o_sj + (sl + 1) * 2048],
            in_=agout[j * 64:(j + 1) * 64, 0:2048]),
          ["agout"], [f"sj{sl}"])
    for j in range(3):
        ldj(j)
    for j in range(8):
        sl = j % 3
        if j + 3 < 8:
            ldj(j + 3)
        E("V", lambda e, j=j: e.tensor_scalar(
            AB[0:32, o_wbc:o_wbc + 64], AB[0:32, o_ones64c:o_ones64c + 64],
            a_ascr[0:32, 700 + j:701 + j], None, ALU.mult),
          ["ones64c", "wT"], ["wbc"])
        for g in range(4):
            bank = pf[4 + g]
            bname = PFB[4 + g]
            E("T", lambda e, g=g, bank=bank: e.matmul(
                bank[0:64, 0:512], AB[0:32, o_wbc:o_wbc + 64],
                c_sel2[:, g * 512:(g + 1) * 512], start=True, stop=True),
              ["wbc", "c_sel2"], [bname])
            if j == 0:
                E("V", lambda e, g=g, sl=sl, bank=bank: e.tensor_tensor(
                    AFp[0:64, o_sinA + g * 512:o_sinA + (g + 1) * 512],
                    AFp[0:64, o_sj + sl * 2048 + g * 512:
                        o_sj + sl * 2048 + (g + 1) * 512],
                    bank[0:64, 0:512], ALU.mult),
                  [f"sj{sl}", bname], ["sinA"])
            else:
                E("V", lambda e, g=g, sl=sl, bank=bank: e.tensor_tensor(
                    AFp[0:64, o_tmp + g * 512:o_tmp + (g + 1) * 512],
                    AFp[0:64, o_sj + sl * 2048 + g * 512:
                        o_sj + sl * 2048 + (g + 1) * 512],
                    bank[0:64, 0:512], ALU.mult),
                  [f"sj{sl}", bname], ["tmpc"])
                E("V", lambda e, g=g: e.tensor_tensor(
                    AFp[0:64, o_sinA + g * 512:o_sinA + (g + 1) * 512],
                    AFp[0:64, o_sinA + g * 512:o_sinA + (g + 1) * 512],
                    AFp[0:64, o_tmp + g * 512:o_tmp + (g + 1) * 512],
                    ALU.add), ["sinA", "tmpc"], ["sinA"])
    E("V", lambda e: ecopy(e, AB[0:64, o_sinb:o_sinb + 2048],
                           AFp[0:64, o_sinA:o_sinA + 2048]),
      ["sinA"], ["sinb"])
    E("V", lambda e: e.memset(AB[64:128, o_sinb:o_sinb + 2048], 0.0),
      [], ["sinbpad"])
    sbt["_o_sinb"] = o_sinb

def emit_p3(ctx, woutbf, u_ext, sp_y, sp_v, sp_z, sp_x1, sp_x0):
    P, E, pf, PFB = ctx["P"], ctx["E"], ctx["pf"], ctx["PFB"]
    arb, arf, AB, AFp, sbt = ctx["arb"], ctx["arf"], ctx["AB"], ctx["AF"], ctx["sbt"]
    c_ident_f = sbt["c_ident_f"]
    c_ones_row = sbt["c_ones_row"]
    c_ones_colb = sbt["c_ones_colb"]
    p_rmsw = sbt["p_rmsw"]
    o_sinb = sbt["_o_sinb"]

    P.barrier()
    arb.reset(2176)
    arf.reset()
    o_wout = arb.take(16 * 2048)
    o_ysl = arb.take(2048)
    o_vsl = arb.take(4096)
    o_zsl = arb.take(2048)
    o_ynb = arb.take(2 * 2048)
    o_x1st = arb.take(2048)
    o_x0st = arb.take(2048)
    E("V", lambda e: e.memset(AB[64:128, o_vsl:o_vsl + 4096], 0.0),
      [], ["vslpad"])
    o_yg = arf.take(2048)
    o_ur = arf.take(2 * 2048)
    o_ln = arf.take(2048)
    o_sqv = arf.take(2048)
    o_x1f = arf.take(2048)
    o_rr = arf.take(128)
    o_cols = arf.take(16)

    def stage_a(tc):
        h = tc % 2
        E("Dw", lambda e, tc=tc: e.dma_start(
            out=AB[:, o_ysl:o_ysl + 2048], in_=sp_y[tc]),
          [f"spy{tc}"], ["ysl"])
        E("Dw", lambda e, tc=tc: e.dma_start(
            out=AB[0:64, o_vsl:o_vsl + 4096], in_=sp_v[tc]),
          [f"spv{tc}"], ["vsl"])
        E("Dw", lambda e, tc=tc: e.dma_start(
            out=AB[:, o_zsl:o_zsl + 2048],
            in_=sp_z[:, :, tc * 128:(tc + 1) * 128].rearrange("a p t -> p a t")),
          [f"spz{i}" for i in range(16)], ["zsl"])
        E("Dw", lambda e, tc=tc, h=h: e.dma_start(
            out=AFp[:, o_ur + h * 2048:o_ur + (h + 1) * 2048],
            in_=u_ext[3 + tc * 128:3 + (tc + 1) * 128, :]), [], [f"ur{h}"])
        for g in range(4):
            bank = pf[g]
            bname = PFB[g]

            def yrmm(e, g=g, bank=bank):
                inst = None
                for q in range(4):
                    jj = g * 4 + q
                    for e_ in range(2):
                        hh = 2 * jj + e_
                        lhs = AB[:, o_sinb + hh * 64:o_sinb + (hh + 1) * 64]
                        rhs = AB[:, o_vsl + hh * 128:o_vsl + (hh + 1) * 128]
                        inst = e.matmul(bank[e_ * 64:(e_ + 1) * 64,
                                             q * 128:(q + 1) * 128],
                                        lhs, rhs, start=True, stop=True)
                return inst
            E("T", yrmm, ["sinb", "sinbpad", "vsl", "vslpad"], [bname])
            E("V", lambda e, g=g, bank=bank: e.tensor_tensor(
                AFp[:, o_yg + g * 512:o_yg + (g + 1) * 512], bank[:, 0:512],
                AB[:, o_ysl + g * 512:o_ysl + (g + 1) * 512],
                ALU.add), [bname, "ysl"], ["yg"])
            E("V", lambda e, g=g: e.tensor_tensor(
                AFp[:, o_yg + g * 512:o_yg + (g + 1) * 512],
                AFp[:, o_yg + g * 512:o_yg + (g + 1) * 512],
                AB[:, o_zsl + g * 512:o_zsl + (g + 1) * 512],
                ALU.mult), ["yg", "zsl"], ["yg"])
        E("S", lambda e: e.activation(AFp[:, o_sqv:o_sqv + 2048],
                                      AFp[:, o_yg:o_yg + 2048], AF.Square),
          ["yg"], ["sqv"])

    def stage_a_sqmm(tc):
        def sqmm(e):
            inst = None
            for jj in range(16):
                inst = e.matmul(pf[4][0:1, 0:128], sbt["c_ones_colf"][:, :],
                                AFp[:, o_sqv + jj * 128:o_sqv + (jj + 1) * 128],
                                start=(jj == 0), stop=(jj == 15))
            return inst
        E("T", sqmm, ["sqv", "c_ones_colf"], [PFB[4]])
        E("V", lambda e: e.tensor_scalar(AFp[0:1, o_rr:o_rr + 128],
                                         pf[4][0:1, 0:128], 1.0 / 2048.0, EPS,
                                         ALU.mult, ALU.add), [PFB[4]], ["rr"])
        E("S", lambda e: e.activation(AFp[0:1, o_rr:o_rr + 128],
                                      AFp[0:1, o_rr:o_rr + 128], AF.Ln),
          ["rr"], ["rr"])
        E("S", lambda e: e.activation(AFp[0:1, o_rr:o_rr + 128],
                                      AFp[0:1, o_rr:o_rr + 128], AF.Exp,
                                      scale=-0.5), ["rr"], ["rr"])

    def stage_a_ynb(tc):
        hy = tc % 2
        E("T", lambda e: e.matmul(pf[5][:, 0:128], c_ones_row[:, :],
                                  AFp[0:1, o_rr:o_rr + 128],
                                  start=True, stop=True),
          ["rr", "c_ones_row"], [PFB[5]])
        for jj in range(16):
            E("V",
              lambda e, jj=jj, hy=hy: e.scalar_tensor_tensor(
                AB[:, o_ynb + hy * 2048 + jj * 128:
                   o_ynb + hy * 2048 + (jj + 1) * 128],
                AFp[:, o_yg + jj * 128:o_yg + (jj + 1) * 128],
                p_rmsw[:, jj:jj + 1], pf[5][:, 0:128], ALU.mult, ALU.mult),
              ["yg", "p_rmsw", PFB[5]], [f"ynb{hy}"])

    def stage_b(tc):
        h = tc % 2
        for nb in range(4):
            bank = pf[nb]
            bname = PFB[nb]

            def womm(e, nb=nb, bank=bank, h=h):
                inst = None
                for ec in range(16):
                    inst = e.matmul(bank[:, 0:512],
                                    AB[:, o_ynb + h * 2048 + ec * 128:
                                       o_ynb + h * 2048 + (ec + 1) * 128],
                                    AB[:, o_wout + ec * 2048 + nb * 512:
                                       o_wout + ec * 2048 + (nb + 1) * 512],
                                    start=(ec == 0), stop=(ec == 15))
                return inst
            E("T", womm, [f"ynb{h}", "woutres"], [bname])
            E("V", lambda e, nb=nb, h=h, bank=bank: e.tensor_tensor(
                AFp[:, o_ln + nb * 512:o_ln + (nb + 1) * 512], bank[:, 0:512],
                AFp[:, o_ur + h * 2048 + nb * 512:o_ur + h * 2048 + (nb + 1) * 512],
                ALU.add), [bname, f"ur{h}"], ["lnb"])

    def stage_b_chain(tc):
        h = tc % 2
        E("V", lambda e: e.reduce_sum(AFp[:, o_cols:o_cols + 1],
                                      AFp[:, o_ln:o_ln + 2048], AX.X),
          ["lnb"], ["mcol"])
        E("V", lambda e: e.tensor_scalar_mul(AFp[:, o_cols:o_cols + 1],
                                             AFp[:, o_cols:o_cols + 1],
                                             1.0 / 2048.0), ["mcol"], ["mcol"])
        E("V", lambda e: e.tensor_scalar(
            AFp[:, o_ln:o_ln + 2048], AFp[:, o_ln:o_ln + 2048],
            AFp[:, o_cols:o_cols + 1], None, ALU.subtract),
          ["lnb", "mcol"], ["lnb"])
        E("S", lambda e: e.activation(AFp[:, o_sqv:o_sqv + 2048],
                                      AFp[:, o_ln:o_ln + 2048], AF.Square),
          ["lnb"], ["sqv"])
        E("V", lambda e: e.reduce_sum(AFp[:, o_cols + 1:o_cols + 2],
                                      AFp[:, o_sqv:o_sqv + 2048], AX.X),
          ["sqv"], ["vcol"])
        E("V", lambda e: e.tensor_scalar(AFp[:, o_cols + 1:o_cols + 2],
                                         AFp[:, o_cols + 1:o_cols + 2],
                                         1.0 / 2048.0, EPS, ALU.mult, ALU.add),
          ["vcol"], ["vcol"])
        E("S", lambda e: e.activation(AFp[:, o_cols + 1:o_cols + 2],
                                      AFp[:, o_cols + 1:o_cols + 2], AF.Ln),
          ["vcol"], ["vcol"])
        E("S", lambda e: e.activation(AFp[:, o_cols + 1:o_cols + 2],
                                      AFp[:, o_cols + 1:o_cols + 2], AF.Exp,
                                      scale=-0.5), ["vcol"], ["vcol"])
        E("V", lambda e: e.tensor_scalar(
            AFp[:, o_x1f:o_x1f + 2048],
            AFp[:, o_ln:o_ln + 2048],
            AFp[:, o_cols + 1:o_cols + 2], None, ALU.mult),
          ["lnb", "vcol"], ["x1f"])
        E("V", lambda e: ecopy(e,
            AB[:, o_x1st:o_x1st + 2048],
            AFp[:, o_x1f:o_x1f + 2048]), ["x1f"], ["x1st"])
        E("Ds", lambda e, tc=tc: e.dma_start(
            out=sp_x1[tc], in_=AB[:, o_x1st:o_x1st + 2048]),
          ["x1st"], [f"spx1_{tc}"])

    def stage_c(tc):
        h = tc % 2
        for g in range(4):
            bank = pf[6 + (g % 2)]
            bname = PFB[6 + (g % 2)]

            def trx0(e, g=g, bank=bank):
                inst = None
                for k in range(4):
                    dk = g * 4 + k
                    inst = e.transpose(bank[:, k * 128:(k + 1) * 128],
                                       AFp[:, o_x1f + dk * 128:
                                           o_x1f + (dk + 1) * 128],
                                       c_ident_f[:, :])
                return inst
            E("T", trx0, ["x1f", "c_ident_f"], [bname])
            E("S" if g % 2 == 0 else "V",
              lambda e, g=g, bank=bank: ecopy(e,
                  AB[:, o_x0st + g * 512:o_x0st + (g + 1) * 512],
                  bank[:, 0:512]), [bname], ["x0st"])
        E("Ds", lambda e, tc=tc: e.dma_start(
            out=sp_x0[:, :, tc * 128:(tc + 1) * 128].rearrange("d p t -> p d t"),
            in_=AB[:, o_x0st:o_x0st + 2048]),
          ["x0st"], [f"spx0_{tc}"])

    for it in range(NCHUNK + 2):
        if it < NCHUNK:
            stage_a(it)
        if 0 <= it - 1 < NCHUNK:
            stage_b(it - 1)
        if it < NCHUNK:
            stage_a_sqmm(it)
        if 0 <= it - 2 < NCHUNK:
            stage_c(it - 2)
        if it < NCHUNK:
            stage_a_ynb(it)
        if 0 <= it - 1 < NCHUNK:
            stage_b_chain(it - 1)


def emit_p4(ctx, wupB, wgB, wdnR, sp_x2, sp_x0, sp_x1, out):
    P, E, pf, PFB = ctx["P"], ctx["E"], ctx["pf"], ctx["PFB"]
    arb, arf, AB, AFp, sbt = ctx["arb"], ctx["arf"], ctx["AB"], ctx["AF"], ctx["sbt"]
    s_hsu16 = sbt["s_hsu16"]
    s_hsu64 = sbt["s_hsu64"]
    p_wsuup = sbt["p_wsuup"]
    p_wsugt = sbt["p_wsugt"]
    c_ones_row = sbt["c_ones_row"]
    a_ascr = sbt["a_ascr"]

    P.barrier()
    arb.reset()
    arf.reset()
    o_x0sl = arb.take(2 * 512)
    o_x0h = arb.take(16 * 512)
    o_x0hm = arb.take(16 * 512)
    o_x2st = arb.take(2 * 512)
    o_x2l = arb.take(4 * 256)
    o_wup = arb.take(2 * 2048)
    o_wg = arb.take(2 * 2048)
    o_wdn = arb.take(4 * 2048)
    o_scr = arb.take(2048)
    o_x1c = arb.take(2 * 2048)
    o_ln = arf.take(2 * 2048)
    o_out = arf.take(2048)
    o_wsdb = arf.take(2048)
    o_cols = arf.take(16)


    wsd_row = ctx["wsd_row_ap"]
    E("Dw", lambda e: e.dma_start(out=AFp[0:1, o_wsdb:o_wsdb + 2048], in_=wsd_row),
      [], ["wsdrow"])
    for i in range(4):
        E("T", lambda e, i=i: e.matmul(
            pf[i][:, :], c_ones_row[:, :],
            AFp[0:1, o_wsdb + i * 512:o_wsdb + (i + 1) * 512],
            start=True, stop=True),
          ["wsdrow", "c_ones_row"], [PFB[i]])
    for i in range(4):
        E("V", lambda e, i=i: e.tensor_scalar_mul(
            AFp[:, o_wsdb + i * 512:o_wsdb + (i + 1) * 512], pf[i][:, :],
            64.0 * CORR64), [PFB[i]], ["p_wsdb", "wsdrow"])

    bounce_done = [f"fw0_{i}" for i in range(64)] + \
                  [f"fw1_{i}" for i in range(64)] + \
                  [f"fw{w}_{i}_{hf}" for w in range(2)
                   for i in range(64, 128) for hf in range(2)]
    dv = wdnR.rearrange("(u ah) (l d) -> u (ah l) d", u=64, ah=2, l=64, d=2048)
    h16m = hadamard(16)

    for s in range(NSUPER):
        for dc in range(16):
            hh = dc % 2
            E("Dw", lambda e, s=s, dc=dc, hh=hh: e.dma_start(
                out=AB[:, o_x0sl + hh * 512:o_x0sl + (hh + 1) * 512],
                in_=sp_x0[dc, :, s * 512:(s + 1) * 512]),
              [f"spx0_{tcx}" for tcx in range(s * 4, s * 4 + 4)], [f"x0sl{hh}"])
            bank = pf[4 + (dc % 2)]
            bname = PFB[4 + (dc % 2)]
            E("T", lambda e, dc=dc, hh=hh, bank=bank: e.matmul(
                bank[:, 0:512], s_hsu16[:, dc * 128:(dc + 1) * 128],
                AB[:, o_x0sl + hh * 512:o_x0sl + (hh + 1) * 512],
                start=True, stop=True),
              [f"x0sl{hh}", "s_hsu16"], [bname])
            E("V" if dc % 2 == 0 else "S",
              lambda e, dc=dc, bank=bank: ecopy(e,
                  AB[:, o_x0h + dc * 512:o_x0h + (dc + 1) * 512], bank[:, 0:512]),
              [bname], ["x0h"])
        for dcp in range(16):
            bank = pf[dcp % 4]
            bname = PFB[dcp % 4]

            def mixmm(e, dcp=dcp, bank=bank):
                inst = None
                for dc in range(16):
                    lhs = sbt["c_m8p"] if h16m[dc, dcp] > 0 else sbt["c_m8n"]
                    inst = e.matmul(bank[:, 0:512], lhs[:, :],
                                    AB[:, o_x0h + dc * 512:o_x0h + (dc + 1) * 512],
                                    start=(dc == 0), stop=(dc == 15))
                return inst
            E("T", mixmm, ["x0h", "c_m8p", "c_m8n"], [bname])
            E("V" if dcp % 2 == 0 else "S",
              lambda e, dcp=dcp, bank=bank: ecopy(e,
                  AB[:, o_x0hm + dcp * 512:o_x0hm + (dcp + 1) * 512],
                  bank[:, 0:512]), [bname], ["x0hm"])

        def emit_hsu64(f, s=s):
            so = 0
            hb = 6 + (f % 2)
            E("T", lambda e, f=f, hb=hb, so=so: e.matmul(
                pf[hb][:, 0:512], s_hsu64[:, f * 128:(f + 1) * 128],
                AB[:, o_scr + so + 1024:o_scr + so + 1536], start=True, stop=True),
              [f"x2p{f%2}", "s_hsu64"], [PFB[hb]])
            E("V" if f % 2 == 0 else "S",
              lambda e, f=f, hb=hb: ecopy(e,
                  AB[:, o_x2st + (f % 2) * 512:o_x2st + (f % 2 + 1) * 512],
                  pf[hb][:, 0:512]), [PFB[hb]], [f"x2st{f%2}"])
            E("Ds", lambda e, f=f, s=s: e.dma_start(
                out=sp_x2[s, f],
                in_=AB[:, o_x2st + (f % 2) * 512:o_x2st + (f % 2 + 1) * 512]),
              [f"x2st{f%2}"], [f"spx2_{s}_{f}"])

        for f in range(64):
            hh = f % 2
            E("Dw", lambda e, f=f, hh=hh: e.dma_start(
                out=AB[:, o_wup + hh * 2048:o_wup + (hh + 1) * 2048],
                in_=wupB[:, f * 2048:(f + 1) * 2048]), bounce_done,
              [f"wupp{hh}"])
            E("Dp", lambda e, f=f, hh=hh: e.dma_start(
                out=AB[:, o_wg + hh * 2048:o_wg + (hh + 1) * 2048],
                in_=wgB[:, f * 2048:(f + 1) * 2048]), bounce_done,
              [f"wgp{hh}"])
            fb = (f % 2) * 2

            def upmm(e, hh=hh, fb=fb):
                inst = None
                for dc in range(16):
                    inst = e.matmul(pf[0 + fb][:, 0:512],
                                    AB[:, o_wup + hh * 2048 + dc * 128:
                                       o_wup + hh * 2048 + (dc + 1) * 128],
                                    AB[:, o_x0hm + dc * 512:o_x0hm + (dc + 1) * 512],
                                    start=(dc == 0), stop=(dc == 15))
                return inst
            E("T", upmm, [f"wupp{hh}", "x0hm"], [PFB[0 + fb]])

            def gtmm(e, hh=hh, fb=fb):
                inst = None
                for dc in range(16):
                    inst = e.matmul(pf[1 + fb][:, 0:512],
                                    AB[:, o_wg + hh * 2048 + dc * 128:
                                       o_wg + hh * 2048 + (dc + 1) * 128],
                                    AB[:, o_x0hm + dc * 512:o_x0hm + (dc + 1) * 512],
                                    start=(dc == 0), stop=(dc == 15))
                return inst
            E("T", gtmm, [f"wgp{hh}", "x0hm"], [PFB[1 + fb]])
            if f > 0:
                emit_hsu64(f - 1)
            so = 0
            E("S", lambda e, f=f, fb=fb, so=so: e.activation(
                AB[:, o_scr + so:o_scr + so + 512], pf[1 + fb][:, 0:512], AF.Silu,
                scale=p_wsugt[:, f:f + 1]),
              [PFB[1 + fb], "p_wsugt"], [f"gsc{f%2}"])
            E("V", lambda e, f=f, fb=fb, so=so: e.tensor_scalar(
                AB[:, o_scr + so + 512:o_scr + so + 1024], pf[0 + fb][:, 0:512],
                p_wsuup[:, f:f + 1], None, ALU.mult),
              [PFB[0 + fb], "p_wsuup"], [f"usc{f%2}"])
            E("V", lambda e, so=so: e.tensor_tensor(
                AB[:, o_scr + so + 1024:o_scr + so + 1536],
                AB[:, o_scr + so:o_scr + so + 512],
                AB[:, o_scr + so + 512:o_scr + so + 1024], ALU.mult),
              [f"gsc{f%2}", f"usc{f%2}"], [f"x2p{f%2}"])
        emit_hsu64(63)
        for tp in range(2):
            def ldw(f, s=s, tp=tp):
                hh = f % 4
                E("Dp", lambda e, f=f, hh=hh: e.dma_start(
                    out=AB[:, o_wdn + hh * 2048:o_wdn + (hh + 1) * 2048],
                    in_=dv[f]), [f"fw2_{pp}" for pp in range(64)], [f"wdnp{hh}"])
                E("Dw", lambda e, f=f, hh=hh, s=s, tp=tp: e.dma_start(
                    out=AB[:, o_x2l + hh * 256:o_x2l + (hh + 1) * 256],
                    in_=sp_x2[s, f, :, tp * 256:(tp + 1) * 256]),
                  [f"spx2_{s}_{f}"], [f"x2l{hh}"])
            for f in range(3):
                ldw(f)
            for f in range(64):
                hh = f % 4
                if f + 3 < 64:
                    ldw(f + 3)

                def dnmm(e, f=f, hh=hh):
                    inst = None
                    for ti in range(2):
                        for nb in range(4):
                            bank = pf[ti * 4 + nb]
                            lhs = AB[:, o_x2l + hh * 256 + ti * 128:
                                     o_x2l + hh * 256 + (ti + 1) * 128]
                            rhs = AB[:, o_wdn + hh * 2048 + nb * 512:
                                     o_wdn + hh * 2048 + (nb + 1) * 512]
                            inst = e.matmul(bank[:, 0:512], lhs, rhs,
                                            start=(f == 0), stop=(f == 63))
                    return inst
                E("T", dnmm, [f"wdnp{hh}", f"x2l{hh}"], [PFB[i] for i in range(8)])
            for ti in range(2):
                tc = s * 4 + tp * 2 + ti
                E("Dw", lambda e, tc=tc, ti=ti: e.dma_start(
                    out=AB[:, o_x1c + ti * 2048:o_x1c + (ti + 1) * 2048],
                    in_=sp_x1[tc]),
                  [f"spx1_{tc}"], [f"x1c{ti}"])
            for ti in range(2):
                for nb in range(4):
                    bank = pf[ti * 4 + nb]
                    bname = PFB[ti * 4 + nb]
                    E("V", lambda e, ti=ti, nb=nb, bank=bank: e.tensor_tensor(
                        AFp[:, o_ln + ti * 2048 + nb * 512:
                            o_ln + ti * 2048 + (nb + 1) * 512],
                        bank[:, 0:512],
                        AFp[:, o_wsdb + nb * 512:o_wsdb + (nb + 1) * 512],
                        ALU.mult), [bname, "p_wsdb"], [f"lnb4_{ti}"])
            for ti in range(2):
                tc = s * 4 + tp * 2 + ti
                ol = o_ln + ti * 2048
                oc = o_cols + ti * 8
                for nb in range(4):
                    E("V", lambda e, ti=ti, nb=nb, ol=ol: e.tensor_tensor(
                        AFp[:, ol + nb * 512:ol + (nb + 1) * 512],
                        AFp[:, ol + nb * 512:ol + (nb + 1) * 512],
                        AB[:, o_x1c + ti * 2048 + nb * 512:
                           o_x1c + ti * 2048 + (nb + 1) * 512],
                        ALU.add), [f"lnb4_{ti}", f"x1c{ti}"], [f"lnb4_{ti}"])
                E("V", lambda e, ol=ol, oc=oc: e.reduce_sum(
                    AFp[:, oc:oc + 1], AFp[:, ol:ol + 2048], AX.X),
                  [f"lnb4_{ti}"], [f"mc2_{ti}"])
                E("V", lambda e, oc=oc: e.tensor_scalar_mul(
                    AFp[:, oc:oc + 1], AFp[:, oc:oc + 1],
                    1.0 / 2048.0), [f"mc2_{ti}"], [f"mc2_{ti}"])
                E("V", lambda e, ol=ol, oc=oc: e.tensor_scalar(
                    AFp[:, ol:ol + 2048], AFp[:, ol:ol + 2048],
                    AFp[:, oc:oc + 1], None, ALU.subtract),
                  [f"lnb4_{ti}", f"mc2_{ti}"], [f"lnb4_{ti}"])
                E("S", lambda e, ol=ol: e.activation(
                    AB[:, o_scr:o_scr + 2048], AFp[:, ol:ol + 2048],
                    AF.Square), [f"lnb4_{ti}"], ["sq2"])
                E("V", lambda e, oc=oc: e.reduce_sum(
                    AFp[:, oc + 1:oc + 2], AB[:, o_scr:o_scr + 2048], AX.X),
                  ["sq2"], [f"vc2_{ti}"])
                E("V", lambda e, oc=oc: e.tensor_scalar(
                    AFp[:, oc + 1:oc + 2], AFp[:, oc + 1:oc + 2],
                    1.0 / 2048.0, EPS, ALU.mult, ALU.add),
                  [f"vc2_{ti}"], [f"vc2_{ti}"])
                E("S", lambda e, oc=oc: e.activation(
                    AFp[:, oc + 1:oc + 2], AFp[:, oc + 1:oc + 2],
                    AF.Ln), [f"vc2_{ti}"], [f"vc2_{ti}"])
                E("S", lambda e, oc=oc: e.activation(
                    AFp[:, oc + 1:oc + 2], AFp[:, oc + 1:oc + 2],
                    AF.Exp, scale=-0.5), [f"vc2_{ti}"], [f"vc2_{ti}"])
                E("V", lambda e, ol=ol, oc=oc: e.tensor_scalar(
                    AFp[:, o_out:o_out + 2048],
                    AFp[:, ol:ol + 2048],
                    AFp[:, oc + 1:oc + 2], None, ALU.mult),
                  [f"lnb4_{ti}", f"vc2_{ti}"], ["outst"])
                E("Dp", lambda e, tc=tc: e.dma_start(
                    out=out[tc * 128:(tc + 1) * 128, :],
                    in_=AFp[:, o_out:o_out + 2048]),
                  ["outst"], [f"outc{tc}"])


_BUILD_CACHE = {}


def kernel(**inputs):
    shared = shared_weight_inputs(inputs)
    consts = host_constants()
    in_maps = []
    for cid in range(NCORES):
        m = {}
        m.update(shared)
        m.update(consts)
        m.update(per_core_inputs(inputs, cid))
        in_maps.append(m)
    nc = build_program()
    res = run_bass_kernel_spmd(nc, in_maps, core_ids=list(range(NCORES)))
    full = np.zeros((2, 8192, 2048), np.float32)
    for cid in range(NCORES):
        full[cid // 4, (cid % 4) * TOK:(cid % 4 + 1) * TOK] = \
            res.results[cid]["out"]
    return full
